# version 13
# baseline (speedup 1.0000x reference)
import numpy as np
from contextlib import ExitStack
import concourse.bass as bass
import concourse.mybir as mybir
from concourse.bass_utils import run_bass_kernel_spmd

F32 = mybir.dt.float32
BF16 = mybir.dt.bfloat16
I32 = mybir.dt.int32
AF = mybir.ActivationFunctionType
ALU = mybir.AluOpType
AX = mybir.AxisListType

T = 2048
D = 2048
NH = 16
HD = 128
II = 8192
NSEQ = 2
NTT = T // 128
NC = D // 128
NG = 4
GH = NH // NG
RSQ = 1.0 / np.sqrt(128.0)
EPS = 1e-5

LVL = {"ln1": 0, "vproj": 1, "qk": 2, "attn": 3, "a3o": 3.2, "xnew": 4, "b1": 5, "full": 9}


class Sem:
    def __init__(self, name):
        self.name = name
        self.h = None
        self.n = 0
    def bump(self, k):
        self.n += k
        return self.n


class Builder:
    def __init__(self, nc):
        self.nc = nc
        self.plan = []
        self.sems = []
        self.scope = "init"
    def sem(self, name):
        s = Sem(name)
        self.sems.append(s)
        return s
    def on(self, eng, fn):
        self.plan.append((eng, fn, self.scope))
    def dma(self, q, out, in_, qsem, transpose=False):
        v = qsem.bump(16)
        def f(e, out=out, in_=in_, tr=transpose, h=qsem):
            e.dma_start(out, in_, transpose=tr).then_inc(h.h, 16)
        self.on(q, f)
        return v
    def wait(self, eng, sem, v):
        if v <= 0:
            return
        def f(e, sem=sem, v=v):
            e.wait_ge(sem.h, v)
        self.on(eng, f)
    def mm(self, ps, lhsT, rhs, start, stop, inc=None, tile_position=None, skip_gc=False):
        v = inc.bump(1) if inc is not None else None
        def f(e, ps=ps, l=lhsT, r=rhs, st=start, sp=stop, inc=inc, tp=tile_position, sgc=skip_gc):
            kw = {}
            if tp is not None:
                kw['tile_position'] = tp
            if sgc:
                kw['skip_group_check'] = True
            i = e.matmul(ps, l, r, start=st, stop=sp, **kw)
            if inc is not None:
                i.then_inc(inc.h, 1)
        self.on('te', f)
        return v
    def act(self, out, in_, func, inc=None, scale=None, bias=None, accum=None):
        v = inc.bump(1) if inc is not None else None
        def f(e, out=out, in_=in_, func=func, inc=inc, scale=scale, bias=bias, accum=accum):
            kw = {}
            if scale is not None: kw['scale'] = scale
            if bias is not None: kw['bias'] = bias
            if accum is not None: kw['accum_out'] = accum
            i = e.activation(out, in_, func, **kw)
            if inc is not None:
                i.then_inc(inc.h, 1)
        self.on('sc', f)
        return v
    def vop(self, eng, name, *args, inc=None, **kw):
        v = inc.bump(1) if inc is not None else None
        def f(e, name=name, args=args, kw=kw, inc=inc):
            i = getattr(e, name)(*args, **kw)
            if inc is not None:
                i.then_inc(inc.h, 1)
        self.on(eng, f)
        return v


def ln_stats(b, st, x_ap, sq_out, n_elts, spe_sve_sac, evac_to=None):
    spe, sve, sac = spe_sve_sac
    if evac_to is not None:
        b.act(evac_to, x_ap, AF.Copy, accum=st['sum'], inc=sac)
    else:
        b.vop('ve', 'reduce_sum', st['sum'], x_ap, AX.X, inc=sve)
    v_sum = sve.n
    b.act(sq_out, x_ap, AF.Square, accum=st['sumsq'], inc=sac)
    v_sq = sac.n
    b.wait('ve', sac, v_sq)
    b.wait('ve', sve, v_sum)
    b.vop('ve', 'tensor_scalar_mul', st['var'], st['sumsq'], 1.0 / n_elts, inc=sve)
    b.vop('ve', 'tensor_scalar_mul', st['m'], st['sum'], 1.0 / n_elts, inc=sve)
    v_mv = sve.n
    b.wait('ve', sve, v_mv)
    b.vop('ve', 'tensor_scalar', st['m2'], st['m'], st['m'], None, ALU.mult, inc=sve)
    v_m2 = sve.n
    b.wait('ve', sve, v_m2)
    b.vop('ve', 'tensor_sub', st['var'], st['var'], st['m2'], inc=sve)
    v_var = sve.n
    b.wait('sc', sve, v_var)
    b.act(st['sd'], st['var'], AF.Sqrt, bias=st['eps'], inc=sac)
    v_sd = sac.n
    b.wait('ve', sac, v_sd)
    b.vop('ve', 'reciprocal', st['r'], st['sd'], inc=sve)
    v_r = sve.n
    b.wait('ve', sve, v_r)
    b.vop('ve', 'tensor_scalar', st['nmr'], st['m'], st['r'], -1.0, ALU.mult, ALU.mult, inc=sve)
    v_nmr = sve.n
    b.wait('ve', sve, v_nmr)
    return sve.n


def build_nc(stage="full"):
    lvl = LVL[stage]
    nc = bass.Bass("TRN2", target_bir_lowering=False)
    b = Builder(nc)

    x_in = nc.dram_tensor("x", [NSEQ, T, D], F32, kind="ExternalInput").ap()
    wq = nc.dram_tensor("wq", [D, D], F32, kind="ExternalInput").ap()
    wk = nc.dram_tensor("wk", [D, D], F32, kind="ExternalInput").ap()
    wv = nc.dram_tensor("wv", [D, D], F32, kind="ExternalInput").ap()
    wo = nc.dram_tensor("wo", [D, D], F32, kind="ExternalInput").ap()
    gw = nc.dram_tensor("gw", [II, D], I32, kind="ExternalInput").ap()
    uw = nc.dram_tensor("uw", [II, D], I32, kind="ExternalInput").ap()
    dw = nc.dram_tensor("dw", [D, II], I32, kind="ExternalInput").ap()
    gs = nc.dram_tensor("gs", [II], F32, kind="ExternalInput").ap()
    us = nc.dram_tensor("us", [II], F32, kind="ExternalInput").ap()
    ds = nc.dram_tensor("ds", [D], F32, kind="ExternalInput").ap()
    out = nc.dram_tensor("out", [NSEQ, T, D], F32, kind="ExternalOutput").ap()

    test_kind = "ExternalOutput" if stage != "full" else "Internal"
    wq_b = nc.dram_tensor("wq_b", [D, D], BF16, kind="Internal").ap()
    wk_b = nc.dram_tensor("wk_b", [D, D], BF16, kind="Internal").ap()
    wv_b = nc.dram_tensor("wv_b", [D, D], BF16, kind="Internal").ap()
    wo_b = nc.dram_tensor("wo_b", [D, D], BF16, kind="Internal").ap()
    wqT = nc.dram_tensor("wqT", [D, D], BF16, kind="Internal").ap()
    wkT = nc.dram_tensor("wkT", [D, D], BF16, kind="Internal").ap()
    wvT = nc.dram_tensor("wvT", [D, D], BF16, kind="Internal").ap()
    woT = nc.dram_tensor("woT", [D, D], BF16, kind="Internal").ap()
    g_raw = nc.dram_tensor("g_raw", [II, D], BF16, kind="Internal").ap()
    u_raw = nc.dram_tensor("u_raw", [II, D], BF16, kind="Internal").ap()
    d_raw = nc.dram_tensor("d_raw", [D, II], BF16, kind="Internal").ap()
    wgT = nc.dram_tensor("wgT", [D, II], BF16, kind="Internal").ap()
    wuT = nc.dram_tensor("wuT", [D, II], BF16, kind="Internal").ap()
    wdT = nc.dram_tensor("wdT", [II, D], BF16, kind="Internal").ap()
    rs_hbm = nc.dram_tensor("rs_hbm", [NSEQ, NH, 4, 512], F32, kind="Internal").ap()
    ln1s = nc.dram_tensor("ln1s", [NSEQ, T, D], BF16, kind=test_kind).ap()
    ln3s = nc.dram_tensor("ln3s", [NSEQ, T, D], BF16, kind="Internal").ap()
    zts = nc.dram_tensor("zts", [NSEQ, NH, HD, T], BF16, kind=test_kind).ap()
    xn = nc.dram_tensor("xn", [NSEQ, T, D], F32, kind=test_kind).ap()
    hs = nc.dram_tensor("hs", [II, NSEQ * T], BF16, kind=test_kind).ap()
    fs = nc.dram_tensor("fs", [D, NSEQ * T], BF16, kind="Internal").ap()
    dbgv = nc.dram_tensor("dbgv", [NSEQ, NG, 128, NTT, 512], BF16,
                          kind="ExternalOutput" if lvl in (1, 2) else "Internal").ap()
    dbgqk = nc.dram_tensor("dbgqk", [NSEQ, NH, 2, 128, T], BF16,
                           kind="ExternalOutput" if lvl == 2 else "Internal").ap()

    gs_r = gs.rearrange("(a p one) -> a p one", p=128, one=1)
    us_r = us.rearrange("(a p one) -> a p one", p=128, one=1)
    ds_r = ds.rearrange("(a p one) -> a p one", p=128, one=1)
    wgT_r = wgT.rearrange("(c p) i -> p c i", p=128)
    wuT_r = wuT.rearrange("(c p) i -> p c i", p=128)
    wdT_r = wdT.rearrange("(S b p) h -> S p b h", b=8, p=128)
    woT_r = woT.rearrange("(c p) o -> p c o", p=128)
    wqT_r = wqT.rearrange("(c p) o -> p c o", p=128)
    wkT_r = wkT.rearrange("(c p) o -> p c o", p=128)
    wvT_r = wvT.rearrange("(c p) o -> p c o", p=128)
    zts_r = [zts[s].rearrange("h p t -> p h t") for s in range(NSEQ)]
    hs_r = hs.rearrange("(cb p) t -> p cb t", p=128)
    fs_r = fs.rearrange("(q ht p) t -> q p ht t", q=4, ht=4, p=128)
    fs_p = fs.rearrange("(pr fg p) t -> pr p fg t", pr=8, fg=2, p=128)

    spe = b.sem("spe"); sac = b.sem("sac"); sve = b.sem("sve"); spl = b.sem("spl")
    c_m = {k: b.sem(f"c_{k}") for k in "qkvo"}
    pa_ld = b.sem("pa_ld")
    pa_s = [b.sem(f"pa_s{i}") for i in range(2)]
    pa_wo = [b.sem(f"pa_wo{i}") for i in range(2)]
    pu_ld = b.sem("pu_ld")
    pu_s = [b.sem(f"pu_s{i}") for i in range(2)]
    ax = [b.sem(f"ax{i}") for i in range(2)]
    al_st = b.sem("al_st")
    al_tt = b.sem("al_tt")
    aw = b.sem("aw")
    az = b.sem("az")
    ars = b.sem("ars")
    aru = b.sem("aru")
    azl = [b.sem(f"azl{i}") for i in range(2)]
    awo = b.sem("awo")
    axr = [b.sem(f"axr{i}") for i in range(2)]
    axn = b.sem("axn")
    al3 = b.sem("al3")
    bl3t = b.sem("bl3t")
    bw = [b.sem(f"bw{i}") for i in range(2)]
    bh = b.sem("bh")
    b2h = [b.sem(f"b2h{i}") for i in range(2)]
    b2w = [b.sem(f"b2w{i}") for i in range(2)]
    bf = b.sem("bf")
    b3f = [b.sem(f"b3f{i}") for i in range(2)]
    b3x = [b.sem(f"b3x{i}") for i in range(2)]
    bo = b.sem("bo")
    dvq = b.sem("dvq")
    fin = b.sem("fin")

    es_root = ExitStack()
    for s in b.sems:
        s.h = es_root.enter_context(nc.semaphore(s.name))

    mask01 = es_root.enter_context(nc.sbuf_tensor("mask01", [128, 128], BF16))
    ones32 = es_root.enter_context(nc.sbuf_tensor("ones32", [128, 32], BF16))
    ones1 = es_root.enter_context(nc.sbuf_tensor("ones1", [1, 128], F32))
    onesT = es_root.enter_context(nc.sbuf_tensor("onesT", [128, 512], BF16))
    rsqT = es_root.enter_context(nc.sbuf_tensor("rsqT", [128, 512], BF16))
    eps_t = es_root.enter_context(nc.sbuf_tensor("eps_t", [128, 1], F32))
    gs_all = es_root.enter_context(nc.sbuf_tensor("gs_all", [128, 64], F32))
    us_all = es_root.enter_context(nc.sbuf_tensor("us_all", [128, 64], F32))
    ds_all = es_root.enter_context(nc.sbuf_tensor("ds_all", [128, 16], F32))
    cm128 = es_root.enter_context(nc.sbuf_tensor("cm128", [1, 128], BF16))
    p_tst = es_root.enter_context(nc.sbuf_tensor("p_tst", [128, 2 * D], BF16))
    stats = []
    for par in range(2):
        d = {}
        for nm in ["sum", "sumsq", "m", "m2", "var", "sd", "r", "nmr"]:
            d[nm] = es_root.enter_context(nc.sbuf_tensor(f"st_{nm}_{par}", [128, 1], F32))[:, :]
        d['eps'] = eps_t[:, :]
        stats.append(d)

    b.vop('ve', 'memset', eps_t[:], EPS)
    b.vop('ve', 'memset', ones32[:], 1.0)
    b.vop('ve', 'memset', ones1[:], 1.0)
    b.vop('ve', 'memset', onesT[:], 1.0)
    b.vop('ve', 'memset', rsqT[:], RSQ)
    b.vop('ve', 'memset', cm128[:], -128.0)
    b.vop('ve', 'memset', mask01[:], 1.0, inc=sve)
    v_mask_set = sve.n
    b.wait('gp', sve, v_mask_set)
    b.vop('gp', 'affine_select', mask01[:], mask01[:], [[1, 128]], ALU.is_ge, 0.0,
          base=0, channel_multiplier=-1, inc=spl)

    b.scope = 'prologue'
    for nm, (src, dst) in dict(q=(wq, wq_b), k=(wk, wk_b), v=(wv, wv_b), o=(wo, wo_b)).items():
        b.dma('gp', dst[:, :], src[:, :], c_m[nm])

    for nm, (srcb, dstT) in dict(q=(wq_b, wqT), k=(wk_b, wkT), v=(wv_b, wvT)).items():
        b.wait('sc', c_m[nm], 16)
        for c in range(NC):
            par = c % 2
            buf = p_tst[:, par * D:par * D + D]
            b.wait('sc', pa_s[par], pa_s[par].n)
            b.dma('sc', buf, srcb[:, c * 128:(c + 1) * 128], pa_ld, transpose=True)
            b.wait('sc', pa_ld, pa_ld.n)
            b.dma('sc', dstT[c * 128:(c + 1) * 128, :], buf, st_sems_unused if False else pa_s[par])
    v_paqkv = [pa_s[i].n for i in range(2)]

    pa_ld2 = b.sem("pa_ld2")
    pa_ld2.h = es_root.enter_context(nc.semaphore("pa_ld2"))

    def emit_wo_pairs():
        b.wait('sy', c_m['o'], 16)
        b.wait('sy', pa_s[0], v_paqkv[0])
        b.wait('sy', pa_s[1], v_paqkv[1])
        for c in range(NC):
            par = c % 2
            buf = p_tst[:, par * D:par * D + D]
            b.wait('sy', pa_wo[par], pa_wo[par].n)
            b.dma('sy', buf, wo_b[:, c * 128:(c + 1) * 128], pa_ld2, transpose=True)
            b.wait('sy', pa_ld2, pa_ld2.n)
            b.dma('sy', woT[c * 128:(c + 1) * 128, :], buf, pa_wo[par])

    c_f = {m: b.sem(f"c_f{m}") for m in "gud"}
    for m in c_f.values():
        m.h = es_root.enter_context(nc.semaphore(m.name))
    b.wait('gp', al_tt, 256)
    b.dma('gp', g_raw[:, :], gw[:, :], c_f['g'])
    b.dma('gp', u_raw[:, :], uw[:, :], c_f['u'])
    b.dma('gp', d_raw[:, :], dw[:, :], c_f['d'])
    scl_ld = b.sem("scl_ld")
    scl_ld.h = es_root.enter_context(nc.semaphore("scl_ld"))
    for blk in range(64):
        b.dma('gp', gs_all[:, blk:blk + 1], gs_r[blk], scl_ld)
        b.dma('gp', us_all[:, blk:blk + 1], us_r[blk], scl_ld)
    for blk in range(16):
        b.dma('gp', ds_all[:, blk:blk + 1], ds_r[blk], scl_ld)
    v_scl = scl_ld.n

    ffn_wt_tasks = []
    for mat, (dsrc, dstT) in dict(g=(g_raw, wgT), u=(u_raw, wuT)).items():
        for c in range(NC):
            for hf in range(2):
                ffn_wt_tasks.append((mat, dsrc[hf * 4096:(hf + 1) * 4096, c * 128:(c + 1) * 128],
                                     dstT[c * 128:(c + 1) * 128, hf * 4096:(hf + 1) * 4096], 4096))
    for ic in range(II // 128):
        ffn_wt_tasks.append(('d', d_raw[:, ic * 128:(ic + 1) * 128],
                             wdT[ic * 128:(ic + 1) * 128, :], D))
    wt_state = {"i": 0, "par": 0}
    esPump = ExitStack()
    p_tstf = esPump.enter_context(nc.sbuf_tensor("p_tstf", [128, 2 * 4096], BF16))

    def pump_ffn_wt(k):
        for _ in range(k):
            i = wt_state["i"]
            if i >= len(ffn_wt_tasks):
                return
            mat, src, dst, width = ffn_wt_tasks[i]
            par = wt_state["par"]
            buf = p_tstf[:, par * 4096:par * 4096 + width]
            b.wait('sy', pu_s[par], pu_s[par].n)
            b.wait('sy', c_f[mat], 16)
            b.dma('sy', buf, src, pu_ld, transpose=True)
            b.wait('sy', pu_ld, pu_ld.n)
            b.dma('sy', dst, buf, pu_s[par])
            wt_state["par"] = 1 - par
            wt_state["i"] = i + 1

    gate = dict(spe=0, sac=0, sve=0, spl=spl.n)

    def snap_gate():
        gate['spe'] = spe.n; gate['sac'] = sac.n; gate['sve'] = sve.n; gate['spl'] = spl.n

    v_xn_store = [0, 0]
    v_ln3_store = [0, 0]

    for s in range(NSEQ):
        esLn1T = ExitStack()
        ln1T = esLn1T.enter_context(nc.sbuf_tensor(f"ln1T{s}", [128, NC, T], BF16))
        esA0 = ExitStack()
        x_sb = [esA0.enter_context(nc.sbuf_tensor(f"x_sb{s}_{i}", [128, D], F32)) for i in range(2)]
        ln1tok = [esA0.enter_context(nc.sbuf_tensor(f"ln1tok{s}_{i}", [128, D], BF16)) for i in range(2)]
        sq_tr = esA0.enter_context(nc.sbuf_tensor(f"sq_tr{s}", [128, D], BF16))
        esA012 = ExitStack()

        b.wait('sy', spe, gate['spe']); b.wait('sy', sac, gate['sac'])
        b.wait('sy', sve, gate['sve'])
        if s > 0:
            b.wait('sy', az, az.n)
            b.wait('sy', axn, axn.n)
            b.wait('sy', al3, al3.n)

        b.scope = f'A0_{s}'
        v_norm_done = [0, 0]
        for ti in range(NTT):
            par = ti % 2
            st = stats[par]
            b.wait('sy', sve, v_norm_done[par])
            b.dma('sy', x_sb[par][:, :], x_in[s, ti * 128:(ti + 1) * 128, :], ax[par])
            vx = ax[par].n
            b.wait('ve', ax[par], vx)
            b.wait('sc', ax[par], vx)
            ln_stats(b, st, x_sb[par][:, :], sq_tr[:, :], D, (spe, sve, sac))
            b.wait('ve', al_st, al_st.n)
            b.vop('ve', 'tensor_scalar', ln1tok[par][:, :], x_sb[par][:, :],
                  st['r'], st['nmr'], ALU.mult, ALU.add, inc=sve)
            v_norm_done[par] = sve.n
            b.wait('sy', sve, v_norm_done[par])
            b.dma('sy', ln1s[s, ti * 128:(ti + 1) * 128, :], ln1tok[par][:, :], al_st)
        b.wait('sy', al_st, al_st.n)
        for c in range(NC):
            b.dma('sy', ln1T[:, c, :], ln1s[s][0:T, c * 128:(c + 1) * 128], al_tt, transpose=True)
        v_ln1T = al_tt.n
        esA0.close()

        if lvl == 0:
            esA012.close()
            esLn1T.close()
            snap_gate()
            continue

        b.scope = f'A12_{s}'
        wqkv_g = esA012.enter_context(nc.sbuf_tensor(f"wqkv{s}", [128, 3 * NC, 512], BF16))
        v_g = esA012.enter_context(nc.sbuf_tensor(f"v_g{s}", [128, NTT, 512], BF16))
        qT = esA012.enter_context(nc.sbuf_tensor(f"qT{s}", [128, T], BF16))
        kT = esA012.enter_context(nc.sbuf_tensor(f"kT{s}", [128, T], BF16))
        probsT = esA012.enter_context(nc.sbuf_tensor(f"probsT{s}", [128, T], BF16))
        zt_sb = esA012.enter_context(nc.sbuf_tensor(f"zt_sb{s}", [128, T], BF16))
        lnls_sb = esA012.enter_context(nc.sbuf_tensor(f"lnls{s}", [128, 512], F32))
        rss_sb = esA012.enter_context(nc.sbuf_tensor(f"rss{s}", [128, 512], F32))
        row_sb = esA012.enter_context(nc.sbuf_tensor(f"row{s}", [1, T], F32))
        rs_b = esA012.enter_context(nc.sbuf_tensor(f"rs_b{s}", [128, T], BF16))
        zt_raw = esA012.enter_context(nc.sbuf_tensor(f"zt_raw{s}", [128, T], BF16))
        esPA = ExitStack()
        zT_ps = esPA.enter_context(nc.psum_tensor(f"zT_ps{s}", [128, T], F32))
        pp = [esPA.enter_context(nc.psum_tensor(f"pp{s}_{i}", [128, 512], F32)) for i in range(2)]
        s_ps = esPA.enter_context(nc.psum_tensor(f"s_ps{s}", [128, 512], F32))

        v_pe_wdone = 0
        v_pe_vdone = 0
        v_ac_vevac_v = 0
        pp_last_exp = [0, 0]
        pp_last_exp_v = [0, 0]
        v_zevac_a2 = 0
        v_sps_free = 0
        v_pe_k1 = 0
        v_rsb_used = 0
        v_pe_qread = 0
        v_dbg_dump = 0
        av_done_span = [0, 0, 0, 0]
        pend_rs = None
        rs_state = {"v_rsb_used": 0, "v_sps_free": 0, "v_pe_k1": 0}

        def emit_rs_tail(p):
            nonlocal v_sps_free, v_pe_k1, v_rsb_used
            b.wait('te', aru, p['v_row'])
            b.wait('te', sac, p['v_lnls'])
            for bc in range(4):
                b.mm(s_ps[:, :], ones1[:, :], row_sb[0:1, bc * 512:(bc + 1) * 512],
                     start=True, stop=True, inc=spe, skip_gc=True)
                v_bc = spe.n
                b.wait('sc', spe, v_bc)
                b.wait('sc', sve, v_rsb_used)
                b.act(rs_b[:, bc * 512:(bc + 1) * 512], s_ps[:, :], AF.Copy, inc=sac)
                if bc < 3:
                    b.wait('te', sac, sac.n)
            v_rsb = sac.n
            v_sps_free = sac.n
            v_pe_k1 = spe.n
            b.wait('ve', sac, v_rsb)
            b.wait('ve', sac, p['v_zraw'])
            b.wait('ve', az, az.n)
            b.vop('ve', 'tensor_tensor', zt_sb[:, :], zt_raw[:, :], rs_b[:, :],
                  ALU.mult, inc=sve)
            v_zmul = sve.n
            v_rsb_used = v_zmul
            b.wait('sc', sve, v_zmul)
            b.dma('sc', zts[s, p['h'], :, :], zt_sb[:, :], az)

        for g in range(NG):
            b.wait('sy', spe, v_pe_wdone)
            b.wait('sy', al_tt, al_tt.n)
            b.wait('sy', pa_s[0], v_paqkv[0])
            b.wait('sy', pa_s[1], v_paqkv[1])
            for mi, wTr in enumerate([wqT_r, wkT_r, wvT_r]):
                b.dma('sy', wqkv_g[:, mi * NC:(mi + 1) * NC, :],
                      wTr[:, :, g * 512:(g + 1) * 512], aw)
            v_wload = aw.n
            if s == 0 and g == 0:
                emit_wo_pairs()
            b.wait('te', spe, v_pe_vdone)
            b.wait('te', aw, v_wload)
            for ti in range(NTT):
                ps = pp[ti % 2]
                b.wait('te', sac, pp_last_exp[ti % 2])
                for c in range(NC):
                    b.mm(ps[:, :], ln1T[:, c, ti * 128:(ti + 1) * 128],
                         wqkv_g[:, 2 * NC + c, :], start=(c == 0), stop=(c == NC - 1),
                         inc=spe if c == NC - 1 else None)
                v_vmm = spe.n
                b.wait('sc', spe, v_vmm)
                if ti == 0 and lvl in (1, 2):
                    b.wait('sc', dvq, dvq.n)
                b.act(v_g[:, ti, :], ps[:, :], AF.Copy, inc=sac)
                pp_last_exp[ti % 2] = sac.n
            v_ac_vevac = sac.n
            if lvl in (1, 2):
                b.wait('sc', sve, sve.n)
                v_dbg_dump = b.dma('sc', dbgv[s, g], v_g[:, :, :], dvq)
            if lvl == 1:
                v_pe_wdone = spe.n
                v_pe_vdone = spe.n
                continue

            for hh in range(GH):
                h = g * GH + hh
                b.wait('sc', spe, v_pe_qread)
                for (wi, dstT, scl) in [(0, qT, RSQ), (1, kT, 1.0)]:
                    for tc4 in range(4):
                        par = tc4 % 2
                        ps = pp[par]
                        b.wait('te', sac, pp_last_exp[par])
                        b.wait('te', sve, pp_last_exp_v[par])
                        colr = slice(tc4 * 512, (tc4 + 1) * 512)
                        for c in range(NC):
                            b.mm(ps[:, :],
                                 wqkv_g[:, wi * NC + c, hh * 128:(hh + 1) * 128],
                                 ln1T[:, c, colr],
                                 start=(c == 0), stop=(c == NC - 1),
                                 inc=spe if c == NC - 1 else None)
                        v_pmm = spe.n
                        b.wait('ve', spe, v_pmm)
                        b.vop('ve', 'tensor_tensor', dstT[:, colr], ps[:, :],
                              rsqT[:, :] if scl != 1.0 else onesT[:, :], ALU.mult, inc=sve)
                        pp_last_exp_v[par] = sve.n
                v_qk_evac_v = sve.n
                if lvl == 2:
                    b.wait('sc', sve, sve.n)
                    b.dma('sc', dbgqk[s, h, 0], qT[:, :], dvq)
                    v_dbg_dump = b.dma('sc', dbgqk[s, h, 1], kT[:, :], dvq)
                    b.wait('ve', dvq, v_dbg_dump)
                    v_pe_qread = spe.n
                    continue
                if pend_rs is not None:
                    emit_rs_tail(pend_rs)
                    pend_rs = None
                spans_all = []
                for j in range(NTT):
                    ci0 = j // 4
                    for ci in range(ci0, 4):
                        a = max(512 * ci, 128 * j)
                        spans_all.append((j, ci, a, 512 * (ci + 1) - a))
                nsp = len(spans_all)
                sc_mm = [0] * nsp
                exp_done = [0] * nsp
                msk_done = [0] * nsp

                def emit_scores(i):
                    j, ci, a, w = spans_all[i]
                    par = i % 2
                    b.wait('te', sac, pp_last_exp[par])
                    b.wait('te', sve, pp_last_exp_v[par])
                    if i == 0:
                        b.wait('te', sve, v_qk_evac_v)
                    b.mm(pp[par][:, 0:w], kT[:, j * 128:(j + 1) * 128], qT[:, a:a + w],
                         start=True, stop=True, inc=spe)
                    sc_mm[i] = spe.n

                def emit_exp(i):
                    j, ci, a, w = spans_all[i]
                    b.wait('sc', spe, max(sc_mm[i], av_done_span[ci]))
                    b.act(probsT[:, a:a + w], pp[i % 2][:, 0:w], AF.Exp, inc=sac)
                    pp_last_exp[i % 2] = sac.n
                    exp_done[i] = sac.n

                def emit_mask(i):
                    j, ci, a, w = spans_all[i]
                    b.wait('ve', sac, exp_done[i])
                    b.vop('ve', 'tensor_tensor', probsT[:, j * 128:(j + 1) * 128],
                          probsT[:, j * 128:(j + 1) * 128], mask01[:, :], ALU.mult,
                          inc=sve)
                    msk_done[i] = sve.n

                def emit_av(i, first):
                    j, ci, a, w = spans_all[i]
                    if first:
                        b.wait('te', sac, v_zevac_a2)
                        b.wait('te', sac, v_sps_free)
                        b.wait('te', sve, v_ac_vevac_v)
                    if msk_done[i]:
                        b.wait('te', sve, msk_done[i])
                    else:
                        b.wait('te', sac, exp_done[i])
                    b.mm(zT_ps[:, a:a + w], v_g[:, j, hh * 128:(hh + 1) * 128],
                         probsT[:, a:a + w], start=(j == 0), stop=(j == 4 * ci + 3),
                         inc=spe)
                    b.mm(s_ps[32 * ci:32 * ci + 32, (a - 512 * ci):(a - 512 * ci) + w],
                         ones32[:, :], probsT[:, a:a + w],
                         start=(j == 0), stop=(j == 4 * ci + 3),
                         inc=spe, tile_position=(0, 32 * ci), skip_gc=True)
                    av_done_span[ci] = spe.n

                emit_scores(0)
                for i in range(nsp):
                    if i + 1 < nsp:
                        emit_scores(i + 1)
                    emit_exp(i)
                    j, ci, a, w = spans_all[i]
                    if ci == j // 4:
                        emit_mask(i)
                    emit_av(i, first=(i == 0))
                v_pe_qread = spe.n
                v_attn_done = spe.n
                b.wait('sc', spe, v_attn_done)
                b.act(lnls_sb[:, :], s_ps[:, :], AF.Ln, inc=sac)
                v_lnls = sac.n
                b.wait('sc', sve, v_rsb_used)
                b.act(zt_raw[:, :], zT_ps[:, :], AF.Copy, inc=sac)
                v_zevac_a2 = sac.n
                b.wait('sc', sac, v_lnls)
                b.act(rss_sb[:, :], lnls_sb[:, :], AF.Exp, scale=-1.0, inc=sac)
                v_rss = sac.n
                b.wait('sc', sac, v_rss)
                b.dma('sc', rs_hbm[s, h], rss_sb[0:128:32, :], ars)
                b.wait('sc', ars, ars.n)
                b.wait('sc', spe, v_pe_k1)
                b.dma('sc', row_sb[0:1, :],
                      rs_hbm[s, h].rearrange("a t -> (a t)").rearrange("(one t) -> one t", one=1), aru)
                pend_rs = dict(h=h, v_row=aru.n, v_lnls=v_lnls, v_zraw=v_zevac_a2)
            v_pe_wdone = spe.n
            v_pe_vdone = spe.n
            pump_ffn_wt(12 if s == 0 else 20)

        if pend_rs is not None:
            emit_rs_tail(pend_rs)
            pend_rs = None
        esPA.close()
        esA012.close()
        esLn1T.close()

        if lvl <= 3:
            snap_gate()
            continue

        b.scope = f'A3_{s}'
        snap_gate()
        esA3 = ExitStack()
        woT_sb = esA3.enter_context(nc.sbuf_tensor(f"woT_sb{s}", [128, NC, D], BF16))
        zt_in = [esA3.enter_context(nc.sbuf_tensor(f"zt_in{s}_{i}", [128, NH, 128], BF16)) for i in range(2)]
        x_re = [esA3.enter_context(nc.sbuf_tensor(f"x_re{s}_{i}", [128, D], F32)) for i in range(2)]
        o_sb = esA3.enter_context(nc.sbuf_tensor(f"oA3_sb{s}", [128, D], F32))
        ln2_sb = esA3.enter_context(nc.sbuf_tensor(f"ln2_sb{s}", [128, D], F32))
        xn_sb = esA3.enter_context(nc.sbuf_tensor(f"xn_sb{s}", [128, D], F32))
        ln3tok = esA3.enter_context(nc.sbuf_tensor(f"ln3tok{s}", [128, D], BF16))
        sq_tr3 = esA3.enter_context(nc.sbuf_tensor(f"sq_tr3{s}", [128, D], BF16))
        esPA3 = ExitStack()
        o_ps = [esPA3.enter_context(nc.psum_tensor(f"o_ps{s}_{i}", [128, D], F32)) for i in range(2)]

        b.wait('sy', spe, gate['spe']); b.wait('sy', sac, gate['sac'])
        b.wait('sy', sve, gate['sve'])
        b.wait('sy', az, az.n)
        b.wait('te', sac, gate['sac'])

        b.wait('sy', pa_wo[0], pa_wo[0].n)
        b.wait('sy', pa_wo[1], pa_wo[1].n)
        b.dma('sy', woT_sb[:, :, :], woT_r[:, :, :], awo)
        vwo = awo.n

        v_zload = [0, 0]
        v_ps_free = [0, 0]
        v_ps_free_v = [0, 0]
        v_xre_free = [0, 0]
        v_o_used = 0
        pend_st = [None]

        def flush_a3_st(s=s):
            p = pend_st[0]
            if p is None:
                return
            pend_st[0] = None
            b.wait('sy', sve, p['v_xn'])
            b.dma('sy', xn[p['s'], p['ti'] * 128:(p['ti'] + 1) * 128, :], xn_sb[:, :], axn)
            v_xn_store[p['s']] = axn.n
            b.wait('sy', sve, p['v_l3'])
            b.dma('sy', ln3s[p['s'], p['ti'] * 128:(p['ti'] + 1) * 128, :], ln3tok[:, :], al3)
            v_ln3_store[p['s']] = al3.n

        for ti in range(NTT):
            par = ti % 2
            st = stats[par]
            b.wait('sy', spe, v_zload[par])
            b.dma('sy', zt_in[par][:, :, :], zts_r[s][:, :, ti * 128:(ti + 1) * 128], azl[par])
            vz = azl[par].n
            b.wait('sy', sve, v_xre_free[par])
            b.dma('sy', x_re[par][:, :], x_in[s, ti * 128:(ti + 1) * 128, :], axr[par])
            vx = axr[par].n
            flush_a3_st()
            b.wait('te', azl[par], vz)
            b.wait('te', awo, vwo)
            b.wait('te', sac, v_ps_free[par])
            b.wait('te', sve, v_ps_free_v[par])
            for c in range(NH):
                for sp in range(4):
                    b.mm(o_ps[par][:, sp * 512:(sp + 1) * 512], zt_in[par][:, c, :],
                         woT_sb[:, c, sp * 512:(sp + 1) * 512],
                         start=(c == 0), stop=(c == NH - 1),
                         inc=spe if (c == NH - 1 and sp == 3) else None)
            v_omm = spe.n
            v_zload[par] = v_omm
            if lvl == 3.2:
                b.wait('ve', spe, v_omm)
                b.wait('ve', axn, axn.n)
                b.vop('ve', 'tensor_scalar', xn_sb[:, :], o_ps[par][:, :],
                      1.0, 0.0, ALU.mult, ALU.add, inc=sve)
                v_o = sve.n
                v_ps_free_v[par] = v_o
                b.wait('sy', sve, v_o)
                b.dma('sy', xn[s, ti * 128:(ti + 1) * 128, :], xn_sb[:, :], axn)
                v_xn_store[s] = axn.n
                continue
            b.wait('ve', spe, v_omm)
            b.wait('sc', spe, v_omm)
            b.wait('sc', sve, v_o_used)
            ln_stats(b, st, o_ps[par][:, :], sq_tr3[:, :], D, (spe, sve, sac),
                     evac_to=o_sb[:, :])
            v_ps_free[par] = sac.n
            b.vop('ve', 'tensor_scalar', ln2_sb[:, :], o_sb[:, :],
                  st['r'], st['nmr'], ALU.mult, ALU.add, inc=sve)
            v_o_used = sve.n
            b.wait('ve', axr[par], vx)
            b.wait('ve', axn, axn.n)
            b.wait('ve', sve, v_o_used)
            b.vop('ve', 'tensor_add', xn_sb[:, :], x_re[par][:, :], ln2_sb[:, :], inc=sve)
            v_xn = sve.n
            v_xre_free[par] = v_xn
            st2 = stats[1 - par]
            b.wait('sc', sve, v_xn)
            ln_stats(b, st2, xn_sb[:, :], sq_tr3[:, :], D, (spe, sve, sac))
            b.wait('ve', al3, al3.n)
            b.vop('ve', 'tensor_scalar', ln3tok[:, :], xn_sb[:, :],
                  st2['r'], st2['nmr'], ALU.mult, ALU.add, inc=sve)
            v_l3 = sve.n
            pend_st[0] = dict(s=s, ti=ti, v_xn=v_xn, v_l3=v_l3)
        flush_a3_st()
        esPA3.close()
        esA3.close()
        snap_gate()

    pump_ffn_wt(len(ffn_wt_tasks))

    if lvl <= 3:
        b.wait('gp', al_st, al_st.n)
        b.wait('gp', az, az.n)
        b.wait('gp', dvq, dvq.n)
        b.wait('gp', pu_s[0], pu_s[0].n)
        b.wait('gp', pu_s[1], pu_s[1].n)
        for s2 in range(NSEQ):
            b.dma('gp', out[s2], x_in[s2], fin)
        b.wait('gp', fin, fin.n)
        finalize(nc, b, es_root)
        return nc

    if lvl in (3.2, 4):
        b.wait('gp', axn, axn.n)
        b.wait('gp', al_st, al_st.n)
        b.wait('gp', pu_s[0], pu_s[0].n)
        b.wait('gp', pu_s[1], pu_s[1].n)
        for s2 in range(NSEQ):
            b.dma('gp', out[s2], xn[s2], fin)
        b.wait('gp', fin, fin.n)
        finalize(nc, b, es_root)
        return nc

    b.scope = 'B1'
    esB1 = ExitStack()
    ln3T = esB1.enter_context(nc.sbuf_tensor("ln3T", [128, NC, T], BF16))
    wg_sb = [esB1.enter_context(nc.sbuf_tensor(f"wg_sb{i}", [128, NC, 512], BF16)) for i in range(2)]
    wu_sb = [esB1.enter_context(nc.sbuf_tensor(f"wu_sb{i}", [128, NC, 512], BF16)) for i in range(2)]
    s_sb = esB1.enter_context(nc.sbuf_tensor("s_sb", [128, 2, 512], BF16))
    u_sb = esB1.enter_context(nc.sbuf_tensor("u_sb", [128, 2, 512], BF16))
    h_sb = [esB1.enter_context(nc.sbuf_tensor(f"h_sb{i}", [128, T], BF16)) for i in range(2)]
    esPB1 = ExitStack()
    gu_ps = [esPB1.enter_context(nc.psum_tensor(f"gu_ps{i}", [128, 512], F32)) for i in range(8)]

    b.wait('sy', spe, gate['spe']); b.wait('sy', sac, gate['sac'])
    b.wait('sy', sve, gate['sve'])
    b.wait('te', sac, gate['sac']); b.wait('te', sve, gate['sve'])
    b.wait('sy', axn, axn.n)
    b.wait('sy', al3, al3.n)
    b.wait('sy', az, az.n)
    b.wait('sy', pu_s[0], pu_s[0].n)
    b.wait('sy', pu_s[1], pu_s[1].n)
    b.wait('sc', scl_ld, v_scl)
    b.wait('ve', scl_ld, v_scl)

    NSUP = II // 512
    v_pe_wg_done = [0, 0]
    v_hstore = [0, 0]
    g_last_silu = [0, 0, 0, 0]
    u_last_mul = [0, 0, 0, 0]
    v_smul = [0, 0]
    for hf in range(NSEQ):
        b.wait('sy', al3, al3.n)
        b.wait('sy', spe, spe.n)
        for c in range(NC):
            b.dma('sy', ln3T[:, c, :], ln3s[hf][0:T, c * 128:(c + 1) * 128], bl3t, transpose=True)
        v_ln3T = bl3t.n
        for si in range(NSUP):
            sbuf_i = si % 2
            b.wait('sy', spe, v_pe_wg_done[sbuf_i])
            b.dma('sy', wg_sb[sbuf_i][:, :, :], wgT_r[:, :, si * 512:(si + 1) * 512], bw[sbuf_i])
            b.dma('sy', wu_sb[sbuf_i][:, :, :], wuT_r[:, :, si * 512:(si + 1) * 512], bw[sbuf_i])
            v_wg_load = bw[sbuf_i].n
            b.wait('te', bw[sbuf_i], v_wg_load)
            b.wait('te', bl3t, v_ln3T)
            for bb in range(4):
                hpar = bb % 2
                for tc in range(4):
                    par4 = (bb * 4 + tc) % 4
                    gp_ps = gu_ps[2 * par4]
                    up_ps = gu_ps[2 * par4 + 1]
                    b.wait('te', sac, g_last_silu[par4])
                    for c in range(NC):
                        b.mm(gp_ps[:, :], wg_sb[sbuf_i][:, c, bb * 128:(bb + 1) * 128],
                             ln3T[:, c, tc * 512:(tc + 1) * 512],
                             start=(c == 0), stop=(c == NC - 1),
                             inc=spe if c == NC - 1 else None)
                    v_gmm = spe.n
                    b.wait('te', sac, u_last_mul[par4])
                    for c in range(NC):
                        b.mm(up_ps[:, :], wu_sb[sbuf_i][:, c, bb * 128:(bb + 1) * 128],
                             ln3T[:, c, tc * 512:(tc + 1) * 512],
                             start=(c == 0), stop=(c == NC - 1),
                             inc=spe if c == NC - 1 else None)
                    v_umm = spe.n
                    spar = tc % 2
                    b.wait('sc', spe, v_gmm)
                    b.wait('sc', sve, v_smul[spar])
                    b.act(s_sb[:, spar, :], gp_ps[:, :], AF.Silu,
                          scale=gs_all[:, si * 4 + bb:si * 4 + bb + 1], inc=sac)
                    g_last_silu[par4] = sac.n
                    b.wait('sc', spe, v_umm)
                    b.act(u_sb[:, spar, :], up_ps[:, :], AF.Copy,
                          scale=us_all[:, si * 4 + bb:si * 4 + bb + 1], inc=sac)
                    u_last_mul[par4] = sac.n
                    v_si = sac.n
                    b.wait('ve', sac, v_si)
                    if tc == 0:
                        b.wait('ve', bh, bh.n)
                    b.vop('ve', 'tensor_tensor', h_sb[hpar][:, tc * 512:(tc + 1) * 512],
                          u_sb[:, spar, :], s_sb[:, spar, :], ALU.mult, inc=sve)
                    v_smul[spar] = sve.n
                v_hmul = sve.n
                b.wait('gp', sve, v_hmul)
                row0 = si * 512 + bb * 128
                b.dma('gp', hs[row0:row0 + 128, hf * T:(hf + 1) * T], h_sb[hpar][:, :], bh)
            v_pe_wg_done[sbuf_i] = spe.n
    esPB1.close()
    esB1.close()
    esPump.close()
    snap_gate()

    if stage == "b1":
        b.wait('gp', bh, bh.n)
        b.wait('gp', axn, axn.n)
        for s2 in range(NSEQ):
            b.dma('gp', out[s2], xn[s2], fin)
        b.wait('gp', fin, fin.n)
        finalize(nc, b, es_root)
        return nc

    b.scope = 'B2'
    esB3 = ExitStack()
    f_tok = [esB3.enter_context(nc.sbuf_tensor(f"f_tok{i}", [128, D], BF16)) for i in range(2)]
    xn_t = esB3.enter_context(nc.sbuf_tensor("xn_t", [128, D], F32))
    ln4_sb = esB3.enter_context(nc.sbuf_tensor("ln4_sb", [128, D], F32))
    o_sb = esB3.enter_context(nc.sbuf_tensor("o_sb", [128, D], F32))
    sq_tr4 = esB3.enter_context(nc.sbuf_tensor("sq_tr4", [128, D], BF16))
    v_bf_oct = []
    v_f_used = [0, 0]
    v_x_used = [0]

    def emit_b3_tile(tt):
        s3, ti3 = tt // NTT, tt % NTT
        par3 = tt % 2
        st3 = stats[par3]
        b.wait('sy', bf, v_bf_oct[tt // 8])
        b.wait('sy', sve, v_f_used[par3])
        b.dma('sy', f_tok[par3][:, :], fs[0:D, tt * 128:(tt + 1) * 128], b3f[par3], transpose=True)
        vf3 = b3f[par3].n
        b.wait('sy', sve, v_x_used[0])
        b.dma('sy', xn_t[:, :], xn[s3, ti3 * 128:(ti3 + 1) * 128, :], b3x[0])
        vx3 = b3x[0].n
        b.wait('ve', b3f[par3], vf3)
        b.wait('sc', b3f[par3], vf3)
        ln_stats(b, st3, f_tok[par3][:, :], sq_tr4[:, :], D, (spe, sve, sac))
        b.vop('ve', 'tensor_scalar', ln4_sb[:, :], f_tok[par3][:, :],
              st3['r'], st3['nmr'], ALU.mult, ALU.add, inc=sve)
        v_f_used[par3] = sve.n
        b.wait('ve', b3x[0], vx3)
        b.wait('ve', bo, bo.n)
        b.wait('ve', sve, v_f_used[par3])
        b.vop('ve', 'tensor_add', o_sb[:, :], xn_t[:, :], ln4_sb[:, :], inc=sve)
        v_add3 = sve.n
        v_x_used[0] = v_add3
        b.wait('sy', sve, v_add3)
        b.dma('sy', out[s3, ti3 * 128:(ti3 + 1) * 128, :], o_sb[:, :], bo)

    esB2 = ExitStack()
    hT_lo = esB2.enter_context(nc.sbuf_tensor("hT_lo", [128, 32, 1024], BF16))
    hT_hi = esB2.enter_context(nc.sbuf_tensor("hT_hi", [128, 32, 1024], BF16))
    hs_rlo = esB2.enter_context(nc.sbuf_tensor("hs_rlo", [1, 1024], BF16))
    hs_rhi = esB2.enter_context(nc.sbuf_tensor("hs_rhi", [1, 1024], BF16))
    wd_sb = [esB2.enter_context(nc.sbuf_tensor(f"wd_sb{i}", [128, 8, 256], BF16)) for i in range(2)]
    fT_sb = [esB2.enter_context(nc.sbuf_tensor(f"fT_sb{i}", [128, 2, 1024], BF16)) for i in range(2)]
    esPB2 = ExitStack()
    f_ps = [[esPB2.enter_context(nc.psum_tensor(f"f_ps{p}_{t}", [128, 1024], F32)) for t in range(2)] for p in range(2)]

    b.wait('sy', spe, gate['spe']); b.wait('sy', sac, gate['sac'])
    b.wait('sy', sve, gate['sve'])
    b.wait('sy', bh, bh.n)
    b.wait('sy', axn, axn.n)
    b.wait('te', sac, gate['sac']); b.wait('te', sve, gate['sve'])
    b.wait('sc', scl_ld, v_scl)
    b.wait('gp', spe, gate['spe']); b.wait('gp', sac, gate['sac'])
    b.wait('gp', sve, gate['sve'])
    b.wait('gp', bh, bh.n)

    NOCT = (NSEQ * T) // 1024
    v_pe_lo_done = 0
    v_pe_hi_done = 0
    v_pe_wd_done = [0, 0]
    v_f_evac = [0, 0]
    v_rlo_used = 0
    v_rhi_used = 0
    wchunk = 0
    for oc in range(NOCT):
        b.wait('sy', spe, v_pe_lo_done)
        b.dma('sy', hT_lo[:, :, :], hs_r[:, 0:32, oc * 1024:(oc + 1) * 1024], b2h[0])
        v_lo = b2h[0].n
        b.wait('sy', spe, v_pe_hi_done)
        b.dma('sy', hT_hi[:, :, :], hs_r[:, 32:64, oc * 1024:(oc + 1) * 1024], b2h[1])
        v_hi = b2h[1].n
        b.wait('te', b2h[0], v_lo)
        b.wait('te', sac, v_f_evac[0])
        for cb in range(32):
            for hf in range(2):
                b.mm(f_ps[0][0][0:1, hf * 512:(hf + 1) * 512], ones32[:, 0:1],
                     hT_lo[:, cb, hf * 512:(hf + 1) * 512],
                     start=(cb == 0), stop=(cb == 31),
                     inc=spe if (cb == 31 and hf == 1) else None)
        v_hsmm_lo = spe.n
        b.wait('ve', spe, v_hsmm_lo)
        b.wait('ve', spe, v_rlo_used)
        for hf in range(2):
            b.vop('ve', 'tensor_tensor', hs_rlo[0:1, hf * 512:(hf + 1) * 512],
                  f_ps[0][0][0:1, hf * 512:(hf + 1) * 512], onesT[0:1, 0:512],
                  ALU.mult, inc=sve if hf == 1 else None)
        v_rlo = sve.n
        for pr in range(8):
            par = pr % 2
            b.wait('te', sac, v_f_evac[par])
            b.wait('te', sve, v_rlo)
            if pr == 1:
                b.wait('te', sve, v_rhi)
            for fg2 in range(2):
                for hf in range(2):
                    b.mm(f_ps[par][fg2][:, hf * 512:(hf + 1) * 512], cm128[:, :],
                         hs_rlo[0:1, hf * 512:(hf + 1) * 512],
                         start=True, stop=False, skip_gc=True)
            for S in range(8):
                wbuf = wchunk % 2
                wchunk += 1
                b.wait('gp', spe, v_pe_wd_done[wbuf])
                b.dma('gp', wd_sb[wbuf][:, :, :], wdT_r[S][:, :, pr * 256:(pr + 1) * 256], b2w[wbuf])
                v_wd_load = b2w[wbuf].n
                if pr == 0 and S == 4:
                    b.wait('te', b2h[1], v_hi)
                    b.wait('te', sac, v_f_evac[1])
                    for cb in range(32):
                        for hf in range(2):
                            b.mm(f_ps[1][0][0:1, hf * 512:(hf + 1) * 512], ones32[:, 0:1],
                                 hT_hi[:, cb, hf * 512:(hf + 1) * 512],
                                 start=(cb == 0), stop=(cb == 31),
                                 inc=spe if (cb == 31 and hf == 1) else None)
                    v_hsmm_hi = spe.n
                    b.wait('ve', spe, v_hsmm_hi)
                    b.wait('ve', spe, v_rhi_used)
                    for hf in range(2):
                        b.vop('ve', 'tensor_tensor', hs_rhi[0:1, hf * 512:(hf + 1) * 512],
                              f_ps[1][0][0:1, hf * 512:(hf + 1) * 512], onesT[0:1, 0:512],
                              ALU.mult, inc=sve if hf == 1 else None)
                    v_rhi = sve.n
                if S == 4:
                    b.wait('te', sve, v_rhi)
                    for fg2 in range(2):
                        for hf in range(2):
                            b.mm(f_ps[par][fg2][:, hf * 512:(hf + 1) * 512], cm128[:, :],
                                 hs_rhi[0:1, hf * 512:(hf + 1) * 512],
                                 start=False, stop=False, skip_gc=True)
                b.wait('te', b2w[wbuf], v_wd_load)
                hsrc = hT_lo if S < 4 else hT_hi
                for ib in range(8):
                    cb = (S % 4) * 8 + ib
                    last = (S == 7 and ib == 7)
                    for fg2 in range(2):
                        for hf in range(2):
                            b.mm(f_ps[par][fg2][:, hf * 512:(hf + 1) * 512],
                                 wd_sb[wbuf][:, ib, fg2 * 128:(fg2 + 1) * 128],
                                 hsrc[:, cb, hf * 512:(hf + 1) * 512],
                                 start=False, stop=last,
                                 inc=spe if (ib == 7 and fg2 == 1 and hf == 1) else None,
                                 skip_gc=True)
                v_pe_wd_done[wbuf] = spe.n
                if pr == 7 and S == 3:
                    v_pe_lo_done = spe.n
                    v_rlo_used = spe.n
            v_dmm = spe.n
            if pr == 7:
                v_pe_hi_done = v_dmm
                v_rhi_used = v_dmm
            b.wait('sc', spe, v_dmm)
            b.wait('sc', bf, bf.n)
            for fg2 in range(2):
                for hf in range(2):
                    b.act(fT_sb[par][:, fg2, hf * 512:(hf + 1) * 512],
                          f_ps[par][fg2][:, hf * 512:(hf + 1) * 512], AF.Copy,
                          scale=ds_all[:, pr * 2 + fg2:pr * 2 + fg2 + 1],
                          inc=sac if (fg2 == 1 and hf == 1) else None)
            v_f_evac[par] = sac.n
            b.wait('sc', sac, v_f_evac[par])
            b.dma('sc', fs_p[pr][:, :, oc * 1024:(oc + 1) * 1024], fT_sb[par][:, :, :], bf)
            if oc >= 1:
                emit_b3_tile(8 * (oc - 1) + pr)
        v_bf_oct.append(bf.n)
    esPB2.close()
    esB2.close()
    snap_gate()

    b.scope = 'B3'
    for tt in range(24, 32):
        emit_b3_tile(tt)

    b.wait('sy', bo, bo.n)
    esB3.close()
    finalize(nc, b, es_root)
    return nc


def finalize(nc, b, es_root):
    from contextlib import ExitStack as _ES
    with nc.Block() as block:
        def emit(name):
            def run(e):
                cur = None
                stk = _ES()
                for eng, fn, sc in b.plan:
                    if eng == name:
                        if sc != cur:
                            stk.close()
                            stk = _ES()
                            stk.enter_context(nc.named_scope(sc))
                            cur = sc
                        fn(e)
                stk.close()
            return run
        block.sync(emit('sy'))
        block.scalar(emit('sc'))
        block.gpsimd(emit('gp'))
        block.vector(emit('ve'))
        block.tensor(emit('te'))
    es_root.close()


def _in_map(inputs, core):
    x = np.ascontiguousarray(np.asarray(inputs["L0_attn_x"])[core * 2:(core + 1) * 2])
    return {
        "x": x,
        "wq": np.asarray(inputs["L0_attn_Wq"]),
        "wk": np.asarray(inputs["L0_attn_Wk"]),
        "wv": np.asarray(inputs["L0_attn_Wv"]),
        "wo": np.asarray(inputs["L0_attn_Wo"]),
        "gw": np.asarray(inputs["L0_ffn_gate_w"]),
        "uw": np.asarray(inputs["L0_ffn_up_w"]),
        "dw": np.asarray(inputs["L0_ffn_down_w"]),
        "gs": np.asarray(inputs["L0_ffn_gate_scale"]),
        "us": np.asarray(inputs["L0_ffn_up_scale"]),
        "ds": np.asarray(inputs["L0_ffn_down_scale"]),
    }


def kernel(**inputs):
    nc = build_nc("full")
    in_maps = [_in_map(inputs, c) for c in range(8)]
    res = run_bass_kernel_spmd(nc, in_maps, core_ids=list(range(8)))
    return np.concatenate([r["out"] for r in res.results], axis=0)


# version 14 (final)
# speedup vs baseline: 1.0565x; 1.0565x over previous
import numpy as np
from contextlib import ExitStack
import concourse.bass as bass
import concourse.mybir as mybir
from concourse.bass_utils import run_bass_kernel_spmd

F32 = mybir.dt.float32
BF16 = mybir.dt.bfloat16
I32 = mybir.dt.int32
AF = mybir.ActivationFunctionType
ALU = mybir.AluOpType
AX = mybir.AxisListType

T = 2048
D = 2048
NH = 16
HD = 128
II = 8192
NSEQ = 2
NTT = T // 128
NC = D // 128
NG = 4
GH = NH // NG
RSQ = 1.0 / np.sqrt(128.0)
EPS = 1e-5

LVL = {"ln1": 0, "vproj": 1, "qk": 2, "attn": 3, "a3o": 3.2, "xnew": 4, "b1": 5, "full": 9}


class Sem:
    def __init__(self, name):
        self.name = name
        self.h = None
        self.n = 0
    def bump(self, k):
        self.n += k
        return self.n


class Builder:
    def __init__(self, nc):
        self.nc = nc
        self.plan = []
        self.sems = []
        self.scope = "init"
    def sem(self, name):
        s = Sem(name)
        self.sems.append(s)
        return s
    def on(self, eng, fn):
        self.plan.append((eng, fn, self.scope))
    def dma(self, q, out, in_, qsem, transpose=False):
        v = qsem.bump(16)
        def f(e, out=out, in_=in_, tr=transpose, h=qsem):
            e.dma_start(out, in_, transpose=tr).then_inc(h.h, 16)
        self.on(q, f)
        return v
    def wait(self, eng, sem, v):
        if v <= 0:
            return
        def f(e, sem=sem, v=v):
            e.wait_ge(sem.h, v)
        self.on(eng, f)
    def mm(self, ps, lhsT, rhs, start, stop, inc=None, tile_position=None, skip_gc=False):
        v = inc.bump(1) if inc is not None else None
        def f(e, ps=ps, l=lhsT, r=rhs, st=start, sp=stop, inc=inc, tp=tile_position, sgc=skip_gc):
            kw = {}
            if tp is not None:
                kw['tile_position'] = tp
            if sgc:
                kw['skip_group_check'] = True
            i = e.matmul(ps, l, r, start=st, stop=sp, **kw)
            if inc is not None:
                i.then_inc(inc.h, 1)
        self.on('te', f)
        return v
    def act(self, out, in_, func, inc=None, scale=None, bias=None, accum=None):
        v = inc.bump(1) if inc is not None else None
        def f(e, out=out, in_=in_, func=func, inc=inc, scale=scale, bias=bias, accum=accum):
            kw = {}
            if scale is not None: kw['scale'] = scale
            if bias is not None: kw['bias'] = bias
            if accum is not None: kw['accum_out'] = accum
            i = e.activation(out, in_, func, **kw)
            if inc is not None:
                i.then_inc(inc.h, 1)
        self.on('sc', f)
        return v
    def vop(self, eng, name, *args, inc=None, **kw):
        v = inc.bump(1) if inc is not None else None
        def f(e, name=name, args=args, kw=kw, inc=inc):
            i = getattr(e, name)(*args, **kw)
            if inc is not None:
                i.then_inc(inc.h, 1)
        self.on(eng, f)
        return v


def ln_stats(b, st, x_ap, sq_out, n_elts, spe_sve_sac, evac_to=None):
    spe, sve, sac = spe_sve_sac
    if evac_to is not None:
        b.act(evac_to, x_ap, AF.Copy, accum=st['sum'], inc=sac)
    else:
        b.vop('ve', 'reduce_sum', st['sum'], x_ap, AX.X, inc=sve)
    v_sum = sve.n
    b.act(sq_out, x_ap, AF.Square, accum=st['sumsq'], inc=sac)
    v_sq = sac.n
    b.wait('ve', sac, v_sq)
    b.wait('ve', sve, v_sum)
    b.vop('ve', 'tensor_scalar_mul', st['var'], st['sumsq'], 1.0 / n_elts, inc=sve)
    b.vop('ve', 'tensor_scalar_mul', st['m'], st['sum'], 1.0 / n_elts, inc=sve)
    v_mv = sve.n
    b.wait('ve', sve, v_mv)
    b.vop('ve', 'tensor_scalar', st['m2'], st['m'], st['m'], None, ALU.mult, inc=sve)
    v_m2 = sve.n
    b.wait('ve', sve, v_m2)
    b.vop('ve', 'tensor_sub', st['var'], st['var'], st['m2'], inc=sve)
    v_var = sve.n
    b.wait('sc', sve, v_var)
    b.act(st['sd'], st['var'], AF.Sqrt, bias=st['eps'], inc=sac)
    v_sd = sac.n
    b.wait('ve', sac, v_sd)
    b.vop('ve', 'reciprocal', st['r'], st['sd'], inc=sve)
    v_r = sve.n
    b.wait('ve', sve, v_r)
    b.vop('ve', 'tensor_scalar', st['nmr'], st['m'], st['r'], -1.0, ALU.mult, ALU.mult, inc=sve)
    v_nmr = sve.n
    b.wait('ve', sve, v_nmr)
    return sve.n


def build_nc(stage="full"):
    lvl = LVL[stage]
    nc = bass.Bass("TRN2", target_bir_lowering=False)
    b = Builder(nc)

    x_in = nc.dram_tensor("x", [NSEQ, T, D], F32, kind="ExternalInput").ap()
    wq = nc.dram_tensor("wq", [D, D], F32, kind="ExternalInput").ap()
    wk = nc.dram_tensor("wk", [D, D], F32, kind="ExternalInput").ap()
    wv = nc.dram_tensor("wv", [D, D], F32, kind="ExternalInput").ap()
    wo = nc.dram_tensor("wo", [D, D], F32, kind="ExternalInput").ap()
    gw = nc.dram_tensor("gw", [II, D], I32, kind="ExternalInput").ap()
    uw = nc.dram_tensor("uw", [II, D], I32, kind="ExternalInput").ap()
    dw = nc.dram_tensor("dw", [D, II], I32, kind="ExternalInput").ap()
    gs = nc.dram_tensor("gs", [II], F32, kind="ExternalInput").ap()
    us = nc.dram_tensor("us", [II], F32, kind="ExternalInput").ap()
    ds = nc.dram_tensor("ds", [D], F32, kind="ExternalInput").ap()
    out = nc.dram_tensor("out", [NSEQ, T, D], F32, kind="ExternalOutput").ap()

    test_kind = "ExternalOutput" if stage != "full" else "Internal"
    wq_b = nc.dram_tensor("wq_b", [D, D], BF16, kind="Internal").ap()
    wk_b = nc.dram_tensor("wk_b", [D, D], BF16, kind="Internal").ap()
    wv_b = nc.dram_tensor("wv_b", [D, D], BF16, kind="Internal").ap()
    wo_b = nc.dram_tensor("wo_b", [D, D], BF16, kind="Internal").ap()
    wqT = nc.dram_tensor("wqT", [D, D], BF16, kind="Internal").ap()
    wkT = nc.dram_tensor("wkT", [D, D], BF16, kind="Internal").ap()
    wvT = nc.dram_tensor("wvT", [D, D], BF16, kind="Internal").ap()
    woT = nc.dram_tensor("woT", [D, D], BF16, kind="Internal").ap()
    g_raw = nc.dram_tensor("g_raw", [II, D], BF16, kind="Internal").ap()
    u_raw = nc.dram_tensor("u_raw", [II, D], BF16, kind="Internal").ap()
    d_raw = nc.dram_tensor("d_raw", [D, II], BF16, kind="Internal").ap()
    wgT = nc.dram_tensor("wgT", [D, II], BF16, kind="Internal").ap()
    wuT = nc.dram_tensor("wuT", [D, II], BF16, kind="Internal").ap()
    wdT = nc.dram_tensor("wdT", [II, D], BF16, kind="Internal").ap()
    rs_hbm = nc.dram_tensor("rs_hbm", [NSEQ, NH, 4, 512], F32, kind="Internal").ap()
    ln1s = nc.dram_tensor("ln1s", [NSEQ, T, D], BF16, kind=test_kind).ap()
    ln3s = nc.dram_tensor("ln3s", [NSEQ, T, D], BF16, kind="Internal").ap()
    zts = nc.dram_tensor("zts", [NSEQ, NH, HD, T], BF16, kind=test_kind).ap()
    xn = nc.dram_tensor("xn", [NSEQ, T, D], F32, kind=test_kind).ap()
    hs = nc.dram_tensor("hs", [II, NSEQ * T], BF16, kind=test_kind).ap()
    fs = nc.dram_tensor("fs", [D, NSEQ * T], BF16, kind="Internal").ap()
    dbgv = nc.dram_tensor("dbgv", [NSEQ, NG, 128, NTT, 512], BF16,
                          kind="ExternalOutput" if lvl in (1, 2) else "Internal").ap()
    dbgqk = nc.dram_tensor("dbgqk", [NSEQ, NH, 2, 128, T], BF16,
                           kind="ExternalOutput" if lvl == 2 else "Internal").ap()

    gs_r = gs.rearrange("(a p one) -> a p one", p=128, one=1)
    us_r = us.rearrange("(a p one) -> a p one", p=128, one=1)
    ds_r = ds.rearrange("(a p one) -> a p one", p=128, one=1)
    wgT_r = wgT.rearrange("(c p) i -> p c i", p=128)
    wuT_r = wuT.rearrange("(c p) i -> p c i", p=128)
    wdT_r = wdT.rearrange("(S b p) h -> S p b h", b=8, p=128)
    woT_r = woT.rearrange("(c p) o -> p c o", p=128)
    wqT_r = wqT.rearrange("(c p) o -> p c o", p=128)
    wkT_r = wkT.rearrange("(c p) o -> p c o", p=128)
    wvT_r = wvT.rearrange("(c p) o -> p c o", p=128)
    zts_r = [zts[s].rearrange("h p t -> p h t") for s in range(NSEQ)]
    hs_r = hs.rearrange("(cb p) t -> p cb t", p=128)
    fs_r = fs.rearrange("(q ht p) t -> q p ht t", q=4, ht=4, p=128)
    fs_p = fs.rearrange("(pr fg p) t -> pr p fg t", pr=8, fg=2, p=128)

    spe = b.sem("spe"); sac = b.sem("sac"); sve = b.sem("sve"); spl = b.sem("spl")
    c_m = {k: b.sem(f"c_{k}") for k in "qkvo"}
    pa_ld = b.sem("pa_ld")
    pa_s = [b.sem(f"pa_s{i}") for i in range(2)]
    pa_wo = [b.sem(f"pa_wo{i}") for i in range(2)]
    pu_ld = b.sem("pu_ld")
    pu_s = [b.sem(f"pu_s{i}") for i in range(2)]
    ax = [b.sem(f"ax{i}") for i in range(2)]
    al_st = b.sem("al_st")
    al_tt = b.sem("al_tt")
    aw = b.sem("aw")
    az = b.sem("az")
    ars = b.sem("ars")
    aru = b.sem("aru")
    azl = [b.sem(f"azl{i}") for i in range(2)]
    awo = b.sem("awo")
    axr = [b.sem(f"axr{i}") for i in range(2)]
    axn = b.sem("axn")
    al3 = b.sem("al3")
    bl3t = b.sem("bl3t")
    bw = [b.sem(f"bw{i}") for i in range(2)]
    bh = b.sem("bh")
    b2h = [b.sem(f"b2h{i}") for i in range(2)]
    b2w = [b.sem(f"b2w{i}") for i in range(2)]
    bf = b.sem("bf")
    b3f = [b.sem(f"b3f{i}") for i in range(2)]
    b3x = [b.sem(f"b3x{i}") for i in range(2)]
    bo = b.sem("bo")
    bo2 = b.sem("bo2")
    dvq = b.sem("dvq")
    fin = b.sem("fin")

    es_root = ExitStack()
    for s in b.sems:
        s.h = es_root.enter_context(nc.semaphore(s.name))

    mask01 = es_root.enter_context(nc.sbuf_tensor("mask01", [128, 128], BF16))
    ones32 = es_root.enter_context(nc.sbuf_tensor("ones32", [128, 32], BF16))
    ones1 = es_root.enter_context(nc.sbuf_tensor("ones1", [1, 128], F32))
    onesT = es_root.enter_context(nc.sbuf_tensor("onesT", [128, 512], BF16))
    rsqT = es_root.enter_context(nc.sbuf_tensor("rsqT", [128, 512], BF16))
    eps_t = es_root.enter_context(nc.sbuf_tensor("eps_t", [128, 1], F32))
    gs_all = es_root.enter_context(nc.sbuf_tensor("gs_all", [128, 64], F32))
    us_all = es_root.enter_context(nc.sbuf_tensor("us_all", [128, 64], F32))
    ds_all = es_root.enter_context(nc.sbuf_tensor("ds_all", [128, 16], F32))
    cm128 = es_root.enter_context(nc.sbuf_tensor("cm128", [1, 128], BF16))
    p_tst = es_root.enter_context(nc.sbuf_tensor("p_tst", [128, 2 * D], BF16))
    stats = []
    for par in range(2):
        d = {}
        for nm in ["sum", "sumsq", "m", "m2", "var", "sd", "r", "nmr"]:
            d[nm] = es_root.enter_context(nc.sbuf_tensor(f"st_{nm}_{par}", [128, 1], F32))[:, :]
        d['eps'] = eps_t[:, :]
        stats.append(d)

    b.vop('ve', 'memset', eps_t[:], EPS)
    b.vop('ve', 'memset', ones32[:], 1.0)
    b.vop('ve', 'memset', ones1[:], 1.0)
    b.vop('ve', 'memset', onesT[:], 1.0)
    b.vop('ve', 'memset', rsqT[:], RSQ)
    b.vop('ve', 'memset', cm128[:], -128.0)
    b.vop('ve', 'memset', mask01[:], 1.0, inc=sve)
    v_mask_set = sve.n
    b.wait('gp', sve, v_mask_set)
    b.vop('gp', 'affine_select', mask01[:], mask01[:], [[1, 128]], ALU.is_ge, 0.0,
          base=0, channel_multiplier=-1, inc=spl)

    b.scope = 'prologue'
    for nm, (src, dst) in dict(q=(wq, wq_b), k=(wk, wk_b), v=(wv, wv_b), o=(wo, wo_b)).items():
        b.dma('gp', dst[:, :], src[:, :], c_m[nm])

    for nm, (srcb, dstT) in dict(q=(wq_b, wqT), k=(wk_b, wkT), v=(wv_b, wvT)).items():
        b.wait('sc', c_m[nm], 16)
        for c in range(NC):
            par = c % 2
            buf = p_tst[:, par * D:par * D + D]
            b.wait('sc', pa_s[par], pa_s[par].n)
            b.dma('sc', buf, srcb[:, c * 128:(c + 1) * 128], pa_ld, transpose=True)
            b.wait('sc', pa_ld, pa_ld.n)
            b.dma('sc', dstT[c * 128:(c + 1) * 128, :], buf, st_sems_unused if False else pa_s[par])
    v_paqkv = [pa_s[i].n for i in range(2)]

    pa_ld2 = b.sem("pa_ld2")
    pa_ld2.h = es_root.enter_context(nc.semaphore("pa_ld2"))

    def emit_wo_pairs():
        b.wait('sy', c_m['o'], 16)
        b.wait('sy', pa_s[0], v_paqkv[0])
        b.wait('sy', pa_s[1], v_paqkv[1])
        for c in range(NC):
            par = c % 2
            buf = p_tst[:, par * D:par * D + D]
            b.wait('sy', pa_wo[par], pa_wo[par].n)
            b.dma('sy', buf, wo_b[:, c * 128:(c + 1) * 128], pa_ld2, transpose=True)
            b.wait('sy', pa_ld2, pa_ld2.n)
            b.dma('sy', woT[c * 128:(c + 1) * 128, :], buf, pa_wo[par])

    c_f = {m: b.sem(f"c_f{m}") for m in "gud"}
    for m in c_f.values():
        m.h = es_root.enter_context(nc.semaphore(m.name))
    b.wait('gp', al_tt, 256)
    b.dma('gp', g_raw[:, :], gw[:, :], c_f['g'])
    b.dma('gp', u_raw[:, :], uw[:, :], c_f['u'])
    b.dma('gp', d_raw[:, :], dw[:, :], c_f['d'])
    scl_ld = b.sem("scl_ld")
    scl_ld.h = es_root.enter_context(nc.semaphore("scl_ld"))
    for blk in range(64):
        b.dma('gp', gs_all[:, blk:blk + 1], gs_r[blk], scl_ld)
        b.dma('gp', us_all[:, blk:blk + 1], us_r[blk], scl_ld)
    for blk in range(16):
        b.dma('gp', ds_all[:, blk:blk + 1], ds_r[blk], scl_ld)
    v_scl = scl_ld.n

    ffn_wt_tasks = []
    for mat, (dsrc, dstT) in dict(g=(g_raw, wgT), u=(u_raw, wuT)).items():
        for c in range(NC):
            for hf in range(2):
                ffn_wt_tasks.append((mat, dsrc[hf * 4096:(hf + 1) * 4096, c * 128:(c + 1) * 128],
                                     dstT[c * 128:(c + 1) * 128, hf * 4096:(hf + 1) * 4096], 4096))
    for ic in range(II // 128):
        ffn_wt_tasks.append(('d', d_raw[:, ic * 128:(ic + 1) * 128],
                             wdT[ic * 128:(ic + 1) * 128, :], D))
    wt_state = {"i": 0, "par": 0}
    esPump = ExitStack()
    p_tstf = esPump.enter_context(nc.sbuf_tensor("p_tstf", [128, 2 * 4096], BF16))

    def pump_ffn_wt(k):
        for _ in range(k):
            i = wt_state["i"]
            if i >= len(ffn_wt_tasks):
                return
            mat, src, dst, width = ffn_wt_tasks[i]
            par = wt_state["par"]
            buf = p_tstf[:, par * 4096:par * 4096 + width]
            b.wait('sy', pu_s[par], pu_s[par].n)
            b.wait('sy', c_f[mat], 16)
            b.dma('sy', buf, src, pu_ld, transpose=True)
            b.wait('sy', pu_ld, pu_ld.n)
            b.dma('sy', dst, buf, pu_s[par])
            wt_state["par"] = 1 - par
            wt_state["i"] = i + 1

    gate = dict(spe=0, sac=0, sve=0, spl=spl.n)

    def snap_gate():
        gate['spe'] = spe.n; gate['sac'] = sac.n; gate['sve'] = sve.n; gate['spl'] = spl.n

    v_xn_store = [0, 0]
    v_ln3_store = [0, 0]

    for s in range(NSEQ):
        esLn1T = ExitStack()
        ln1T = esLn1T.enter_context(nc.sbuf_tensor(f"ln1T{s}", [128, NC, T], BF16))
        esA0 = ExitStack()
        x_sb = [esA0.enter_context(nc.sbuf_tensor(f"x_sb{s}_{i}", [128, D], F32)) for i in range(2)]
        ln1tok = [esA0.enter_context(nc.sbuf_tensor(f"ln1tok{s}_{i}", [128, D], BF16)) for i in range(2)]
        sq_tr = esA0.enter_context(nc.sbuf_tensor(f"sq_tr{s}", [128, D], BF16))
        esA012 = ExitStack()

        b.wait('sy', spe, gate['spe']); b.wait('sy', sac, gate['sac'])
        b.wait('sy', sve, gate['sve'])
        if s > 0:
            b.wait('sy', az, az.n)
            b.wait('sy', axn, axn.n)
            b.wait('sy', al3, al3.n)

        b.scope = f'A0_{s}'
        v_norm_done = [0, 0]
        for ti in range(NTT):
            par = ti % 2
            st = stats[par]
            b.wait('sy', sve, v_norm_done[par])
            b.dma('sy', x_sb[par][:, :], x_in[s, ti * 128:(ti + 1) * 128, :], ax[par])
            vx = ax[par].n
            b.wait('ve', ax[par], vx)
            b.wait('sc', ax[par], vx)
            ln_stats(b, st, x_sb[par][:, :], sq_tr[:, :], D, (spe, sve, sac))
            b.wait('ve', al_st, al_st.n)
            b.vop('ve', 'tensor_scalar', ln1tok[par][:, :], x_sb[par][:, :],
                  st['r'], st['nmr'], ALU.mult, ALU.add, inc=sve)
            v_norm_done[par] = sve.n
            b.wait('sy', sve, v_norm_done[par])
            b.dma('sy', ln1s[s, ti * 128:(ti + 1) * 128, :], ln1tok[par][:, :], al_st)
        b.wait('sy', al_st, al_st.n)
        for c in range(NC):
            b.dma('sy', ln1T[:, c, :], ln1s[s][0:T, c * 128:(c + 1) * 128], al_tt, transpose=True)
        v_ln1T = al_tt.n
        esA0.close()

        if lvl == 0:
            esA012.close()
            esLn1T.close()
            snap_gate()
            continue

        b.scope = f'A12_{s}'
        wqkv_g = esA012.enter_context(nc.sbuf_tensor(f"wqkv{s}", [128, 3 * NC, 512], BF16))
        v_g = esA012.enter_context(nc.sbuf_tensor(f"v_g{s}", [128, NTT, 512], BF16))
        qT = esA012.enter_context(nc.sbuf_tensor(f"qT{s}", [128, T], BF16))
        kT = esA012.enter_context(nc.sbuf_tensor(f"kT{s}", [128, T], BF16))
        probsT = esA012.enter_context(nc.sbuf_tensor(f"probsT{s}", [128, T], BF16))
        zt_sb = esA012.enter_context(nc.sbuf_tensor(f"zt_sb{s}", [128, T], BF16))
        lnls_sb = esA012.enter_context(nc.sbuf_tensor(f"lnls{s}", [128, 512], F32))
        rss_sb = esA012.enter_context(nc.sbuf_tensor(f"rss{s}", [128, 512], F32))
        row_sb = esA012.enter_context(nc.sbuf_tensor(f"row{s}", [1, T], F32))
        rs_b = esA012.enter_context(nc.sbuf_tensor(f"rs_b{s}", [128, T], BF16))
        zt_raw = esA012.enter_context(nc.sbuf_tensor(f"zt_raw{s}", [128, T], BF16))
        esPA = ExitStack()
        zT_ps = esPA.enter_context(nc.psum_tensor(f"zT_ps{s}", [128, T], F32))
        pp = [esPA.enter_context(nc.psum_tensor(f"pp{s}_{i}", [128, 512], F32)) for i in range(2)]
        s_ps = esPA.enter_context(nc.psum_tensor(f"s_ps{s}", [128, 512], F32))

        v_pe_wdone = 0
        v_pe_vdone = 0
        v_ac_vevac_v = 0
        pp_last_exp = [0, 0]
        pp_last_exp_v = [0, 0]
        v_zevac_a2 = 0
        v_sps_free = 0
        v_pe_k1 = 0
        v_rsb_used = 0
        v_pe_qread = 0
        v_dbg_dump = 0
        av_done_span = [0, 0, 0, 0]
        pend_rs = None
        rs_state = {"v_rsb_used": 0, "v_sps_free": 0, "v_pe_k1": 0}

        def emit_rs_tail(p):
            nonlocal v_sps_free, v_pe_k1, v_rsb_used
            b.wait('te', aru, p['v_row'])
            b.wait('te', sac, p['v_lnls'])
            for bc in range(4):
                b.mm(s_ps[:, :], ones1[:, :], row_sb[0:1, bc * 512:(bc + 1) * 512],
                     start=True, stop=True, inc=spe, skip_gc=True)
                v_bc = spe.n
                b.wait('sc', spe, v_bc)
                b.wait('sc', sve, v_rsb_used)
                b.act(rs_b[:, bc * 512:(bc + 1) * 512], s_ps[:, :], AF.Copy, inc=sac)
                if bc < 3:
                    b.wait('te', sac, sac.n)
            v_rsb = sac.n
            v_sps_free = sac.n
            v_pe_k1 = spe.n
            b.wait('ve', sac, v_rsb)
            b.wait('ve', sac, p['v_zraw'])
            b.wait('ve', az, az.n)
            b.vop('ve', 'tensor_tensor', zt_sb[:, :], zt_raw[:, :], rs_b[:, :],
                  ALU.mult, inc=sve)
            v_zmul = sve.n
            v_rsb_used = v_zmul
            b.wait('sc', sve, v_zmul)
            b.dma('sc', zts[s, p['h'], :, :], zt_sb[:, :], az)

        for g in range(NG):
            b.wait('sy', spe, v_pe_wdone)
            b.wait('sy', al_tt, al_tt.n)
            b.wait('sy', pa_s[0], v_paqkv[0])
            b.wait('sy', pa_s[1], v_paqkv[1])
            for mi, wTr in enumerate([wqT_r, wkT_r, wvT_r]):
                b.dma('sy', wqkv_g[:, mi * NC:(mi + 1) * NC, :],
                      wTr[:, :, g * 512:(g + 1) * 512], aw)
            v_wload = aw.n
            if s == 0 and g == 0:
                emit_wo_pairs()
            b.wait('te', spe, v_pe_vdone)
            b.wait('te', aw, v_wload)
            for ti in range(NTT):
                ps = pp[ti % 2]
                b.wait('te', sac, pp_last_exp[ti % 2])
                for c in range(NC):
                    b.mm(ps[:, :], ln1T[:, c, ti * 128:(ti + 1) * 128],
                         wqkv_g[:, 2 * NC + c, :], start=(c == 0), stop=(c == NC - 1),
                         inc=spe if c == NC - 1 else None)
                v_vmm = spe.n
                b.wait('sc', spe, v_vmm)
                if ti == 0 and lvl in (1, 2):
                    b.wait('sc', dvq, dvq.n)
                b.act(v_g[:, ti, :], ps[:, :], AF.Copy, inc=sac)
                pp_last_exp[ti % 2] = sac.n
            v_ac_vevac = sac.n
            if lvl in (1, 2):
                b.wait('sc', sve, sve.n)
                v_dbg_dump = b.dma('sc', dbgv[s, g], v_g[:, :, :], dvq)
            if lvl == 1:
                v_pe_wdone = spe.n
                v_pe_vdone = spe.n
                continue

            for hh in range(GH):
                h = g * GH + hh
                b.wait('sc', spe, v_pe_qread)
                for (wi, dstT, scl) in [(0, qT, RSQ), (1, kT, 1.0)]:
                    for tc4 in range(4):
                        par = tc4 % 2
                        ps = pp[par]
                        b.wait('te', sac, pp_last_exp[par])
                        b.wait('te', sve, pp_last_exp_v[par])
                        colr = slice(tc4 * 512, (tc4 + 1) * 512)
                        for c in range(NC):
                            b.mm(ps[:, :],
                                 wqkv_g[:, wi * NC + c, hh * 128:(hh + 1) * 128],
                                 ln1T[:, c, colr],
                                 start=(c == 0), stop=(c == NC - 1),
                                 inc=spe if c == NC - 1 else None)
                        v_pmm = spe.n
                        b.wait('ve', spe, v_pmm)
                        b.vop('ve', 'tensor_tensor', dstT[:, colr], ps[:, :],
                              rsqT[:, :] if scl != 1.0 else onesT[:, :], ALU.mult, inc=sve)
                        pp_last_exp_v[par] = sve.n
                v_qk_evac_v = sve.n
                if lvl == 2:
                    b.wait('sc', sve, sve.n)
                    b.dma('sc', dbgqk[s, h, 0], qT[:, :], dvq)
                    v_dbg_dump = b.dma('sc', dbgqk[s, h, 1], kT[:, :], dvq)
                    b.wait('ve', dvq, v_dbg_dump)
                    v_pe_qread = spe.n
                    continue
                if pend_rs is not None:
                    emit_rs_tail(pend_rs)
                    pend_rs = None
                spans_all = []
                for j in range(NTT):
                    ci0 = j // 4
                    for ci in range(ci0, 4):
                        a = max(512 * ci, 128 * j)
                        spans_all.append((j, ci, a, 512 * (ci + 1) - a))
                nsp = len(spans_all)
                sc_mm = [0] * nsp
                exp_done = [0] * nsp
                msk_done = [0] * nsp

                def emit_scores(i):
                    j, ci, a, w = spans_all[i]
                    par = i % 2
                    b.wait('te', sac, pp_last_exp[par])
                    b.wait('te', sve, pp_last_exp_v[par])
                    if i == 0:
                        b.wait('te', sve, v_qk_evac_v)
                    b.mm(pp[par][:, 0:w], kT[:, j * 128:(j + 1) * 128], qT[:, a:a + w],
                         start=True, stop=True, inc=spe)
                    sc_mm[i] = spe.n

                def emit_exp(i):
                    j, ci, a, w = spans_all[i]
                    b.wait('sc', spe, max(sc_mm[i], av_done_span[ci]))
                    b.act(probsT[:, a:a + w], pp[i % 2][:, 0:w], AF.Exp, inc=sac)
                    pp_last_exp[i % 2] = sac.n
                    exp_done[i] = sac.n

                def emit_mask(i):
                    j, ci, a, w = spans_all[i]
                    b.wait('ve', sac, exp_done[i])
                    b.vop('ve', 'tensor_tensor', probsT[:, j * 128:(j + 1) * 128],
                          probsT[:, j * 128:(j + 1) * 128], mask01[:, :], ALU.mult,
                          inc=sve)
                    msk_done[i] = sve.n

                def emit_av(i, first):
                    j, ci, a, w = spans_all[i]
                    if first:
                        b.wait('te', sac, v_zevac_a2)
                        b.wait('te', sac, v_sps_free)
                        b.wait('te', sve, v_ac_vevac_v)
                    if msk_done[i]:
                        b.wait('te', sve, msk_done[i])
                    else:
                        b.wait('te', sac, exp_done[i])
                    b.mm(zT_ps[:, a:a + w], v_g[:, j, hh * 128:(hh + 1) * 128],
                         probsT[:, a:a + w], start=(j == 0), stop=(j == 4 * ci + 3),
                         inc=spe)
                    b.mm(s_ps[32 * ci:32 * ci + 32, (a - 512 * ci):(a - 512 * ci) + w],
                         ones32[:, :], probsT[:, a:a + w],
                         start=(j == 0), stop=(j == 4 * ci + 3),
                         inc=spe, tile_position=(0, 32 * ci), skip_gc=True)
                    av_done_span[ci] = spe.n

                emit_scores(0)
                for i in range(nsp):
                    if i + 1 < nsp:
                        emit_scores(i + 1)
                    emit_exp(i)
                    j, ci, a, w = spans_all[i]
                    if ci == j // 4:
                        emit_mask(i)
                    emit_av(i, first=(i == 0))
                v_pe_qread = spe.n
                v_attn_done = spe.n
                b.wait('sc', spe, v_attn_done)
                b.act(lnls_sb[:, :], s_ps[:, :], AF.Ln, inc=sac)
                v_lnls = sac.n
                b.wait('sc', sve, v_rsb_used)
                b.act(zt_raw[:, :], zT_ps[:, :], AF.Copy, inc=sac)
                v_zevac_a2 = sac.n
                b.wait('sc', sac, v_lnls)
                b.act(rss_sb[:, :], lnls_sb[:, :], AF.Exp, scale=-1.0, inc=sac)
                v_rss = sac.n
                b.wait('sc', sac, v_rss)
                b.dma('sc', rs_hbm[s, h], rss_sb[0:128:32, :], ars)
                b.wait('sc', ars, ars.n)
                b.wait('sc', spe, v_pe_k1)
                b.dma('sc', row_sb[0:1, :],
                      rs_hbm[s, h].rearrange("a t -> (a t)").rearrange("(one t) -> one t", one=1), aru)
                pend_rs = dict(h=h, v_row=aru.n, v_lnls=v_lnls, v_zraw=v_zevac_a2)
            v_pe_wdone = spe.n
            v_pe_vdone = spe.n
            pump_ffn_wt(12 if s == 0 else 20)

        if pend_rs is not None:
            emit_rs_tail(pend_rs)
            pend_rs = None
        esPA.close()
        esA012.close()
        esLn1T.close()

        if lvl <= 3:
            snap_gate()
            continue

        b.scope = f'A3_{s}'
        snap_gate()
        esA3 = ExitStack()
        woT_sb = esA3.enter_context(nc.sbuf_tensor(f"woT_sb{s}", [128, NC, D], BF16))
        zt_in = [esA3.enter_context(nc.sbuf_tensor(f"zt_in{s}_{i}", [128, NH, 128], BF16)) for i in range(2)]
        x_re = [esA3.enter_context(nc.sbuf_tensor(f"x_re{s}_{i}", [128, D], F32)) for i in range(2)]
        o_sb = esA3.enter_context(nc.sbuf_tensor(f"oA3_sb{s}", [128, D], F32))
        ln2_sb = esA3.enter_context(nc.sbuf_tensor(f"ln2_sb{s}", [128, D], F32))
        xn_sb = esA3.enter_context(nc.sbuf_tensor(f"xn_sb{s}", [128, D], F32))
        ln3tok = esA3.enter_context(nc.sbuf_tensor(f"ln3tok{s}", [128, D], BF16))
        sq_tr3 = esA3.enter_context(nc.sbuf_tensor(f"sq_tr3{s}", [128, D], BF16))
        esPA3 = ExitStack()
        o_ps = [esPA3.enter_context(nc.psum_tensor(f"o_ps{s}_{i}", [128, D], F32)) for i in range(2)]

        b.wait('sy', spe, gate['spe']); b.wait('sy', sac, gate['sac'])
        b.wait('sy', sve, gate['sve'])
        b.wait('sy', az, az.n)
        b.wait('te', sac, gate['sac'])

        b.wait('sy', pa_wo[0], pa_wo[0].n)
        b.wait('sy', pa_wo[1], pa_wo[1].n)
        b.dma('sy', woT_sb[:, :, :], woT_r[:, :, :], awo)
        vwo = awo.n

        v_zload = [0, 0]
        v_ps_free = [0, 0]
        v_ps_free_v = [0, 0]
        v_xre_free = [0, 0]
        v_o_used = 0
        pend_st = [None]

        def flush_a3_st(s=s):
            p = pend_st[0]
            if p is None:
                return
            pend_st[0] = None
            b.wait('sy', sve, p['v_xn'])
            b.dma('sy', xn[p['s'], p['ti'] * 128:(p['ti'] + 1) * 128, :], xn_sb[:, :], axn)
            v_xn_store[p['s']] = axn.n
            b.wait('sy', sve, p['v_l3'])
            b.dma('sy', ln3s[p['s'], p['ti'] * 128:(p['ti'] + 1) * 128, :], ln3tok[:, :], al3)
            v_ln3_store[p['s']] = al3.n

        for ti in range(NTT):
            par = ti % 2
            st = stats[par]
            b.wait('sy', spe, v_zload[par])
            b.dma('sy', zt_in[par][:, :, :], zts_r[s][:, :, ti * 128:(ti + 1) * 128], azl[par])
            vz = azl[par].n
            b.wait('sy', sve, v_xre_free[par])
            b.dma('sy', x_re[par][:, :], x_in[s, ti * 128:(ti + 1) * 128, :], axr[par])
            vx = axr[par].n
            flush_a3_st()
            b.wait('te', azl[par], vz)
            b.wait('te', awo, vwo)
            b.wait('te', sac, v_ps_free[par])
            b.wait('te', sve, v_ps_free_v[par])
            for c in range(NH):
                for sp in range(4):
                    b.mm(o_ps[par][:, sp * 512:(sp + 1) * 512], zt_in[par][:, c, :],
                         woT_sb[:, c, sp * 512:(sp + 1) * 512],
                         start=(c == 0), stop=(c == NH - 1),
                         inc=spe if (c == NH - 1 and sp == 3) else None)
            v_omm = spe.n
            v_zload[par] = v_omm
            if lvl == 3.2:
                b.wait('ve', spe, v_omm)
                b.wait('ve', axn, axn.n)
                b.vop('ve', 'tensor_scalar', xn_sb[:, :], o_ps[par][:, :],
                      1.0, 0.0, ALU.mult, ALU.add, inc=sve)
                v_o = sve.n
                v_ps_free_v[par] = v_o
                b.wait('sy', sve, v_o)
                b.dma('sy', xn[s, ti * 128:(ti + 1) * 128, :], xn_sb[:, :], axn)
                v_xn_store[s] = axn.n
                continue
            b.wait('ve', spe, v_omm)
            b.wait('sc', spe, v_omm)
            b.wait('sc', sve, v_o_used)
            ln_stats(b, st, o_ps[par][:, :], sq_tr3[:, :], D, (spe, sve, sac),
                     evac_to=o_sb[:, :])
            v_ps_free[par] = sac.n
            b.vop('ve', 'tensor_scalar', ln2_sb[:, :], o_sb[:, :],
                  st['r'], st['nmr'], ALU.mult, ALU.add, inc=sve)
            v_o_used = sve.n
            b.wait('ve', axr[par], vx)
            b.wait('ve', axn, axn.n)
            b.wait('ve', sve, v_o_used)
            b.vop('ve', 'tensor_add', xn_sb[:, :], x_re[par][:, :], ln2_sb[:, :], inc=sve)
            v_xn = sve.n
            v_xre_free[par] = v_xn
            st2 = stats[1 - par]
            b.wait('sc', sve, v_xn)
            ln_stats(b, st2, xn_sb[:, :], sq_tr3[:, :], D, (spe, sve, sac))
            b.wait('ve', al3, al3.n)
            b.vop('ve', 'tensor_scalar', ln3tok[:, :], xn_sb[:, :],
                  st2['r'], st2['nmr'], ALU.mult, ALU.add, inc=sve)
            v_l3 = sve.n
            pend_st[0] = dict(s=s, ti=ti, v_xn=v_xn, v_l3=v_l3)
        flush_a3_st()
        esPA3.close()
        esA3.close()
        snap_gate()

    pump_ffn_wt(len(ffn_wt_tasks))

    if lvl <= 3:
        b.wait('gp', al_st, al_st.n)
        b.wait('gp', az, az.n)
        b.wait('gp', dvq, dvq.n)
        b.wait('gp', pu_s[0], pu_s[0].n)
        b.wait('gp', pu_s[1], pu_s[1].n)
        for s2 in range(NSEQ):
            b.dma('gp', out[s2], x_in[s2], fin)
        b.wait('gp', fin, fin.n)
        finalize(nc, b, es_root)
        return nc

    if lvl in (3.2, 4):
        b.wait('gp', axn, axn.n)
        b.wait('gp', al_st, al_st.n)
        b.wait('gp', pu_s[0], pu_s[0].n)
        b.wait('gp', pu_s[1], pu_s[1].n)
        for s2 in range(NSEQ):
            b.dma('gp', out[s2], xn[s2], fin)
        b.wait('gp', fin, fin.n)
        finalize(nc, b, es_root)
        return nc

    b.scope = 'B1'
    esB1 = ExitStack()
    ln3T = esB1.enter_context(nc.sbuf_tensor("ln3T", [128, NC, T], BF16))
    wg_sb = [esB1.enter_context(nc.sbuf_tensor(f"wg_sb{i}", [128, NC, 512], BF16)) for i in range(2)]
    wu_sb = [esB1.enter_context(nc.sbuf_tensor(f"wu_sb{i}", [128, NC, 512], BF16)) for i in range(2)]
    s_sb = esB1.enter_context(nc.sbuf_tensor("s_sb", [128, 2, 512], BF16))
    u_sb = esB1.enter_context(nc.sbuf_tensor("u_sb", [128, 2, 512], BF16))
    h_sb = [esB1.enter_context(nc.sbuf_tensor(f"h_sb{i}", [128, T], BF16)) for i in range(2)]
    esPB1 = ExitStack()
    gu_ps = [esPB1.enter_context(nc.psum_tensor(f"gu_ps{i}", [128, 512], F32)) for i in range(8)]

    b.wait('sy', spe, gate['spe']); b.wait('sy', sac, gate['sac'])
    b.wait('sy', sve, gate['sve'])
    b.wait('te', sac, gate['sac']); b.wait('te', sve, gate['sve'])
    b.wait('sy', axn, axn.n)
    b.wait('sy', al3, al3.n)
    b.wait('sy', az, az.n)
    b.wait('sy', pu_s[0], pu_s[0].n)
    b.wait('sy', pu_s[1], pu_s[1].n)
    b.wait('sc', scl_ld, v_scl)
    b.wait('ve', scl_ld, v_scl)

    NSUP = II // 512
    v_pe_wg_done = [0, 0]
    v_hstore = [0, 0]
    g_last_silu = [0, 0, 0, 0]
    u_last_mul = [0, 0, 0, 0]
    v_smul = [0, 0]
    for hf in range(NSEQ):
        b.wait('sy', al3, al3.n)
        b.wait('sy', spe, spe.n)
        for c in range(NC):
            b.dma('sy', ln3T[:, c, :], ln3s[hf][0:T, c * 128:(c + 1) * 128], bl3t, transpose=True)
        v_ln3T = bl3t.n
        for si in range(NSUP):
            sbuf_i = si % 2
            b.wait('sy', spe, v_pe_wg_done[sbuf_i])
            b.dma('sy', wg_sb[sbuf_i][:, :, :], wgT_r[:, :, si * 512:(si + 1) * 512], bw[sbuf_i])
            b.dma('sy', wu_sb[sbuf_i][:, :, :], wuT_r[:, :, si * 512:(si + 1) * 512], bw[sbuf_i])
            v_wg_load = bw[sbuf_i].n
            b.wait('te', bw[sbuf_i], v_wg_load)
            b.wait('te', bl3t, v_ln3T)
            for bb in range(4):
                hpar = bb % 2
                for tc in range(4):
                    par4 = (bb * 4 + tc) % 4
                    gp_ps = gu_ps[2 * par4]
                    up_ps = gu_ps[2 * par4 + 1]
                    b.wait('te', sac, g_last_silu[par4])
                    for c in range(NC):
                        b.mm(gp_ps[:, :], wg_sb[sbuf_i][:, c, bb * 128:(bb + 1) * 128],
                             ln3T[:, c, tc * 512:(tc + 1) * 512],
                             start=(c == 0), stop=(c == NC - 1),
                             inc=spe if c == NC - 1 else None)
                    v_gmm = spe.n
                    b.wait('te', sac, u_last_mul[par4])
                    for c in range(NC):
                        b.mm(up_ps[:, :], wu_sb[sbuf_i][:, c, bb * 128:(bb + 1) * 128],
                             ln3T[:, c, tc * 512:(tc + 1) * 512],
                             start=(c == 0), stop=(c == NC - 1),
                             inc=spe if c == NC - 1 else None)
                    v_umm = spe.n
                    spar = tc % 2
                    b.wait('sc', spe, v_gmm)
                    b.wait('sc', sve, v_smul[spar])
                    b.act(s_sb[:, spar, :], gp_ps[:, :], AF.Silu,
                          scale=gs_all[:, si * 4 + bb:si * 4 + bb + 1], inc=sac)
                    g_last_silu[par4] = sac.n
                    b.wait('sc', spe, v_umm)
                    b.act(u_sb[:, spar, :], up_ps[:, :], AF.Copy,
                          scale=us_all[:, si * 4 + bb:si * 4 + bb + 1], inc=sac)
                    u_last_mul[par4] = sac.n
                    v_si = sac.n
                    b.wait('ve', sac, v_si)
                    if tc == 0:
                        b.wait('ve', bh, bh.n)
                    b.vop('ve', 'tensor_tensor', h_sb[hpar][:, tc * 512:(tc + 1) * 512],
                          u_sb[:, spar, :], s_sb[:, spar, :], ALU.mult, inc=sve)
                    v_smul[spar] = sve.n
                v_hmul = sve.n
                b.wait('gp', sve, v_hmul)
                row0 = si * 512 + bb * 128
                b.dma('gp', hs[row0:row0 + 128, hf * T:(hf + 1) * T], h_sb[hpar][:, :], bh)
            v_pe_wg_done[sbuf_i] = spe.n
    esPB1.close()
    esB1.close()
    esPump.close()
    snap_gate()

    if stage == "b1":
        b.wait('gp', bh, bh.n)
        b.wait('gp', axn, axn.n)
        for s2 in range(NSEQ):
            b.dma('gp', out[s2], xn[s2], fin)
        b.wait('gp', fin, fin.n)
        finalize(nc, b, es_root)
        return nc

    b.scope = 'B2'
    esB3 = ExitStack()
    f_tok = [esB3.enter_context(nc.sbuf_tensor(f"f_tok{i}", [128, D], BF16)) for i in range(2)]
    xn_t = esB3.enter_context(nc.sbuf_tensor("xn_t", [128, D], F32))
    ln4_sb = esB3.enter_context(nc.sbuf_tensor("ln4_sb", [128, D], F32))
    o_sb = [esB3.enter_context(nc.sbuf_tensor(f"o_sb{i}", [128, D], F32)) for i in range(2)]
    v_o_used3 = [0, 0]
    sq_tr4 = esB3.enter_context(nc.sbuf_tensor("sq_tr4", [128, D], BF16))
    v_bf_oct = []
    v_f_used = [0, 0]
    v_x_used = [0]

    def emit_b3_tile(tt):
        s3, ti3 = tt // NTT, tt % NTT
        par3 = tt % 2
        st3 = stats[par3]
        b.wait('sy', bf, v_bf_oct[tt // 8])
        b.wait('sy', sve, v_f_used[par3])
        b.dma('sy', f_tok[par3][:, :], fs[0:D, tt * 128:(tt + 1) * 128], b3f[par3], transpose=True)
        vf3 = b3f[par3].n
        b.wait('sy', sve, v_x_used[0])
        b.dma('sy', xn_t[:, :], xn[s3, ti3 * 128:(ti3 + 1) * 128, :], b3x[0])
        vx3 = b3x[0].n
        b.wait('ve', b3f[par3], vf3)
        b.wait('sc', b3f[par3], vf3)
        ln_stats(b, st3, f_tok[par3][:, :], sq_tr4[:, :], D, (spe, sve, sac))
        b.vop('ve', 'tensor_scalar', ln4_sb[:, :], f_tok[par3][:, :],
              st3['r'], st3['nmr'], ALU.mult, ALU.add, inc=sve)
        v_f_used[par3] = sve.n
        bo_p = bo if par3 == 0 else bo2
        b.wait('ve', b3x[0], vx3)
        b.wait('ve', bo_p, v_o_used3[par3])
        b.wait('ve', sve, v_f_used[par3])
        b.vop('ve', 'tensor_add', o_sb[par3][:, :], xn_t[:, :], ln4_sb[:, :], inc=sve)
        v_add3 = sve.n
        v_x_used[0] = v_add3
        b.wait('sy', sve, v_add3)
        b.dma('sy', out[s3, ti3 * 128:(ti3 + 1) * 128, :], o_sb[par3][:, :], bo_p)
        v_o_used3[par3] = bo_p.n

    esB2 = ExitStack()
    hT_lo = esB2.enter_context(nc.sbuf_tensor("hT_lo", [128, 32, 1024], BF16))
    hT_hi = esB2.enter_context(nc.sbuf_tensor("hT_hi", [128, 32, 1024], BF16))
    hs_rlo = esB2.enter_context(nc.sbuf_tensor("hs_rlo", [1, 1024], BF16))
    hs_rhi = esB2.enter_context(nc.sbuf_tensor("hs_rhi", [1, 1024], BF16))
    wd_sb = [esB2.enter_context(nc.sbuf_tensor(f"wd_sb{i}", [128, 8, 256], BF16)) for i in range(2)]
    fT_sb = [esB2.enter_context(nc.sbuf_tensor(f"fT_sb{i}", [128, 2, 1024], BF16)) for i in range(2)]
    esPB2 = ExitStack()
    f_ps = [[esPB2.enter_context(nc.psum_tensor(f"f_ps{p}_{t}", [128, 1024], F32)) for t in range(2)] for p in range(2)]

    b.wait('sy', spe, gate['spe']); b.wait('sy', sac, gate['sac'])
    b.wait('sy', sve, gate['sve'])
    b.wait('sy', bh, bh.n)
    b.wait('sy', axn, axn.n)
    b.wait('te', sac, gate['sac']); b.wait('te', sve, gate['sve'])
    b.wait('sc', scl_ld, v_scl)
    b.wait('gp', spe, gate['spe']); b.wait('gp', sac, gate['sac'])
    b.wait('gp', sve, gate['sve'])
    b.wait('gp', bh, bh.n)

    NOCT = (NSEQ * T) // 1024
    v_pe_lo_done = 0
    v_pe_hi_done = 0
    v_pe_wd_done = [0, 0]
    v_f_evac = [0, 0]
    v_rlo_used = 0
    v_rhi_used = 0
    wchunk = 0
    for oc in range(NOCT):
        b.wait('sy', spe, v_pe_lo_done)
        b.dma('sy', hT_lo[:, :, :], hs_r[:, 0:32, oc * 1024:(oc + 1) * 1024], b2h[0])
        v_lo = b2h[0].n
        b.wait('sy', spe, v_pe_hi_done)
        b.dma('sy', hT_hi[:, :, :], hs_r[:, 32:64, oc * 1024:(oc + 1) * 1024], b2h[1])
        v_hi = b2h[1].n
        b.wait('te', b2h[0], v_lo)
        b.wait('te', sac, v_f_evac[0])
        for cb in range(32):
            for hf in range(2):
                b.mm(f_ps[0][0][0:1, hf * 512:(hf + 1) * 512], ones32[:, 0:1],
                     hT_lo[:, cb, hf * 512:(hf + 1) * 512],
                     start=(cb == 0), stop=(cb == 31),
                     inc=spe if (cb == 31 and hf == 1) else None)
        v_hsmm_lo = spe.n
        b.wait('ve', spe, v_hsmm_lo)
        b.wait('ve', spe, v_rlo_used)
        for hf in range(2):
            b.vop('ve', 'tensor_tensor', hs_rlo[0:1, hf * 512:(hf + 1) * 512],
                  f_ps[0][0][0:1, hf * 512:(hf + 1) * 512], onesT[0:1, 0:512],
                  ALU.mult, inc=sve if hf == 1 else None)
        v_rlo = sve.n
        for pr in range(8):
            par = pr % 2
            b.wait('te', sac, v_f_evac[par])
            b.wait('te', sve, v_rlo)
            if pr == 1:
                b.wait('te', sve, v_rhi)
            for fg2 in range(2):
                for hf in range(2):
                    b.mm(f_ps[par][fg2][:, hf * 512:(hf + 1) * 512], cm128[:, :],
                         hs_rlo[0:1, hf * 512:(hf + 1) * 512],
                         start=True, stop=False, skip_gc=True)
            for S in range(8):
                wbuf = wchunk % 2
                wchunk += 1
                b.wait('gp', spe, v_pe_wd_done[wbuf])
                b.dma('gp', wd_sb[wbuf][:, :, :], wdT_r[S][:, :, pr * 256:(pr + 1) * 256], b2w[wbuf])
                v_wd_load = b2w[wbuf].n
                if pr == 0 and S == 4:
                    b.wait('te', b2h[1], v_hi)
                    b.wait('te', sac, v_f_evac[1])
                    for cb in range(32):
                        for hf in range(2):
                            b.mm(f_ps[1][0][0:1, hf * 512:(hf + 1) * 512], ones32[:, 0:1],
                                 hT_hi[:, cb, hf * 512:(hf + 1) * 512],
                                 start=(cb == 0), stop=(cb == 31),
                                 inc=spe if (cb == 31 and hf == 1) else None)
                    v_hsmm_hi = spe.n
                    b.wait('ve', spe, v_hsmm_hi)
                    b.wait('ve', spe, v_rhi_used)
                    for hf in range(2):
                        b.vop('ve', 'tensor_tensor', hs_rhi[0:1, hf * 512:(hf + 1) * 512],
                              f_ps[1][0][0:1, hf * 512:(hf + 1) * 512], onesT[0:1, 0:512],
                              ALU.mult, inc=sve if hf == 1 else None)
                    v_rhi = sve.n
                if S == 4:
                    b.wait('te', sve, v_rhi)
                    for fg2 in range(2):
                        for hf in range(2):
                            b.mm(f_ps[par][fg2][:, hf * 512:(hf + 1) * 512], cm128[:, :],
                                 hs_rhi[0:1, hf * 512:(hf + 1) * 512],
                                 start=False, stop=False, skip_gc=True)
                b.wait('te', b2w[wbuf], v_wd_load)
                hsrc = hT_lo if S < 4 else hT_hi
                for ib in range(8):
                    cb = (S % 4) * 8 + ib
                    last = (S == 7 and ib == 7)
                    for fg2 in range(2):
                        for hf in range(2):
                            b.mm(f_ps[par][fg2][:, hf * 512:(hf + 1) * 512],
                                 wd_sb[wbuf][:, ib, fg2 * 128:(fg2 + 1) * 128],
                                 hsrc[:, cb, hf * 512:(hf + 1) * 512],
                                 start=False, stop=last,
                                 inc=spe if (ib == 7 and fg2 == 1 and hf == 1) else None,
                                 skip_gc=True)
                v_pe_wd_done[wbuf] = spe.n
                if pr == 7 and S == 3:
                    v_pe_lo_done = spe.n
                    v_rlo_used = spe.n
            v_dmm = spe.n
            if pr == 7:
                v_pe_hi_done = v_dmm
                v_rhi_used = v_dmm
            b.wait('sc', spe, v_dmm)
            b.wait('sc', bf, bf.n)
            for fg2 in range(2):
                for hf in range(2):
                    b.act(fT_sb[par][:, fg2, hf * 512:(hf + 1) * 512],
                          f_ps[par][fg2][:, hf * 512:(hf + 1) * 512], AF.Copy,
                          scale=ds_all[:, pr * 2 + fg2:pr * 2 + fg2 + 1],
                          inc=sac if (fg2 == 1 and hf == 1) else None)
            v_f_evac[par] = sac.n
            b.wait('sc', sac, v_f_evac[par])
            b.dma('sc', fs_p[pr][:, :, oc * 1024:(oc + 1) * 1024], fT_sb[par][:, :, :], bf)
            if oc >= 1:
                emit_b3_tile(8 * (oc - 1) + pr)
        v_bf_oct.append(bf.n)
    esPB2.close()
    esB2.close()
    snap_gate()

    b.scope = 'B3'
    for tt in range(24, 32):
        emit_b3_tile(tt)

    b.wait('sy', bo, bo.n)
    b.wait('sy', bo2, bo2.n)
    esB3.close()
    finalize(nc, b, es_root)
    return nc


def finalize(nc, b, es_root):
    from contextlib import ExitStack as _ES
    with nc.Block() as block:
        def emit(name):
            def run(e):
                cur = None
                stk = _ES()
                for eng, fn, sc in b.plan:
                    if eng == name:
                        if sc != cur:
                            stk.close()
                            stk = _ES()
                            stk.enter_context(nc.named_scope(sc))
                            cur = sc
                        fn(e)
                stk.close()
            return run
        block.sync(emit('sy'))
        block.scalar(emit('sc'))
        block.gpsimd(emit('gp'))
        block.vector(emit('ve'))
        block.tensor(emit('te'))
    es_root.close()


def _in_map(inputs, core):
    x = np.ascontiguousarray(np.asarray(inputs["L0_attn_x"])[core * 2:(core + 1) * 2])
    return {
        "x": x,
        "wq": np.asarray(inputs["L0_attn_Wq"]),
        "wk": np.asarray(inputs["L0_attn_Wk"]),
        "wv": np.asarray(inputs["L0_attn_Wv"]),
        "wo": np.asarray(inputs["L0_attn_Wo"]),
        "gw": np.asarray(inputs["L0_ffn_gate_w"]),
        "uw": np.asarray(inputs["L0_ffn_up_w"]),
        "dw": np.asarray(inputs["L0_ffn_down_w"]),
        "gs": np.asarray(inputs["L0_ffn_gate_scale"]),
        "us": np.asarray(inputs["L0_ffn_up_scale"]),
        "ds": np.asarray(inputs["L0_ffn_down_scale"]),
    }


def kernel(**inputs):
    nc = build_nc("full")
    in_maps = [_in_map(inputs, c) for c in range(8)]
    res = run_bass_kernel_spmd(nc, in_maps, core_ids=list(range(8)))
    return np.concatenate([r["out"] for r in res.results], axis=0)


# version 20
# speedup vs baseline: 1.1628x; 1.1006x over previous
import numpy as np
from contextlib import ExitStack
import concourse.bass as bass
import concourse.mybir as mybir
from concourse.bass_utils import run_bass_kernel_spmd

F32 = mybir.dt.float32
BF16 = mybir.dt.bfloat16
I32 = mybir.dt.int32
AF = mybir.ActivationFunctionType
ALU = mybir.AluOpType
AX = mybir.AxisListType

T = 2048
D = 2048
NH = 16
HD = 128
II = 8192
NSEQ = 2
NTT = T // 128
NC = D // 128
NG = 4
GH = NH // NG
RSQ = 1.0 / np.sqrt(128.0)
EPS = 1e-5

LVL = {"ln1": 0, "vproj": 1, "qk": 2, "attn": 3, "a3o": 3.2, "xnew": 4, "b1": 5, "full": 9}


class Sem:
    def __init__(self, name):
        self.name = name
        self.h = None
        self.n = 0
    def bump(self, k):
        self.n += k
        return self.n


class Builder:
    def __init__(self, nc):
        self.nc = nc
        self.plan = []
        self.sems = []
        self.scope = "init"
    def sem(self, name):
        s = Sem(name)
        self.sems.append(s)
        return s
    def on(self, eng, fn):
        self.plan.append((eng, fn, self.scope))
    def dma(self, q, out, in_, qsem, transpose=False):
        v = qsem.bump(16)
        def f(e, out=out, in_=in_, tr=transpose, h=qsem):
            e.dma_start(out, in_, transpose=tr).then_inc(h.h, 16)
        self.on(q, f)
        return v
    def wait(self, eng, sem, v):
        if v <= 0:
            return
        def f(e, sem=sem, v=v):
            e.wait_ge(sem.h, v)
        self.on(eng, f)
    def mm(self, ps, lhsT, rhs, start, stop, inc=None, tile_position=None, skip_gc=False):
        v = inc.bump(1) if inc is not None else None
        def f(e, ps=ps, l=lhsT, r=rhs, st=start, sp=stop, inc=inc, tp=tile_position, sgc=skip_gc):
            kw = {}
            if tp is not None:
                kw['tile_position'] = tp
            if sgc:
                kw['skip_group_check'] = True
            i = e.matmul(ps, l, r, start=st, stop=sp, **kw)
            if inc is not None:
                i.then_inc(inc.h, 1)
        self.on('te', f)
        return v
    def act(self, out, in_, func, inc=None, scale=None, bias=None, accum=None):
        v = inc.bump(1) if inc is not None else None
        def f(e, out=out, in_=in_, func=func, inc=inc, scale=scale, bias=bias, accum=accum):
            kw = {}
            if scale is not None: kw['scale'] = scale
            if bias is not None: kw['bias'] = bias
            if accum is not None: kw['accum_out'] = accum
            i = e.activation(out, in_, func, **kw)
            if inc is not None:
                i.then_inc(inc.h, 1)
        self.on('sc', f)
        return v
    def vop(self, eng, name, *args, inc=None, **kw):
        v = inc.bump(1) if inc is not None else None
        def f(e, name=name, args=args, kw=kw, inc=inc):
            i = getattr(e, name)(*args, **kw)
            if inc is not None:
                i.then_inc(inc.h, 1)
        self.on(eng, f)
        return v


def ln_stats(b, st, x_ap, sq_out, n_elts, spe_sve_sac, evac_to=None):
    spe, sve, sac = spe_sve_sac
    if evac_to is not None:
        b.act(evac_to, x_ap, AF.Copy, accum=st['sum'], inc=sac)
    else:
        b.vop('ve', 'reduce_sum', st['sum'], x_ap, AX.X, inc=sve)
    v_sum = sve.n
    b.act(sq_out, x_ap, AF.Square, accum=st['sumsq'], inc=sac)
    v_sq = sac.n
    b.wait('ve', sac, v_sq)
    b.wait('ve', sve, v_sum)
    b.vop('ve', 'tensor_scalar_mul', st['var'], st['sumsq'], 1.0 / n_elts, inc=sve)
    b.vop('ve', 'tensor_scalar_mul', st['m'], st['sum'], 1.0 / n_elts, inc=sve)
    v_mv = sve.n
    b.wait('ve', sve, v_mv)
    b.vop('ve', 'tensor_scalar', st['m2'], st['m'], st['m'], None, ALU.mult, inc=sve)
    v_m2 = sve.n
    b.wait('ve', sve, v_m2)
    b.vop('ve', 'tensor_sub', st['var'], st['var'], st['m2'], inc=sve)
    v_var = sve.n
    b.wait('sc', sve, v_var)
    b.act(st['sd'], st['var'], AF.Sqrt, bias=st['eps'], inc=sac)
    v_sd = sac.n
    b.wait('ve', sac, v_sd)
    b.vop('ve', 'reciprocal', st['r'], st['sd'], inc=sve)
    v_r = sve.n
    b.wait('ve', sve, v_r)
    b.vop('ve', 'tensor_scalar', st['nmr'], st['m'], st['r'], -1.0, ALU.mult, ALU.mult, inc=sve)
    v_nmr = sve.n
    b.wait('ve', sve, v_nmr)
    return sve.n


def build_nc(stage="full"):
    lvl = LVL[stage]
    nc = bass.Bass("TRN2", target_bir_lowering=False)
    b = Builder(nc)

    x_in = nc.dram_tensor("x", [NSEQ, T, D], F32, kind="ExternalInput").ap()
    wq = nc.dram_tensor("wq", [D, D], F32, kind="ExternalInput").ap()
    wk = nc.dram_tensor("wk", [D, D], F32, kind="ExternalInput").ap()
    wv = nc.dram_tensor("wv", [D, D], F32, kind="ExternalInput").ap()
    wo = nc.dram_tensor("wo", [D, D], F32, kind="ExternalInput").ap()
    gw = nc.dram_tensor("gw", [II, D], I32, kind="ExternalInput").ap()
    uw = nc.dram_tensor("uw", [II, D], I32, kind="ExternalInput").ap()
    dw = nc.dram_tensor("dw", [D, II], I32, kind="ExternalInput").ap()
    gs = nc.dram_tensor("gs", [II], F32, kind="ExternalInput").ap()
    us = nc.dram_tensor("us", [II], F32, kind="ExternalInput").ap()
    ds = nc.dram_tensor("ds", [D], F32, kind="ExternalInput").ap()
    out = nc.dram_tensor("out", [NSEQ, T, D], F32, kind="ExternalOutput").ap()

    test_kind = "ExternalOutput" if stage != "full" else "Internal"
    wq_b = nc.dram_tensor("wq_b", [D, D], BF16, kind="Internal").ap()
    wk_b = nc.dram_tensor("wk_b", [D, D], BF16, kind="Internal").ap()
    wv_b = nc.dram_tensor("wv_b", [D, D], BF16, kind="Internal").ap()
    wo_b = nc.dram_tensor("wo_b", [D, D], BF16, kind="Internal").ap()
    wqT = nc.dram_tensor("wqT", [D, D], BF16, kind="Internal").ap()
    wkT = nc.dram_tensor("wkT", [D, D], BF16, kind="Internal").ap()
    wvT = nc.dram_tensor("wvT", [D, D], BF16, kind="Internal").ap()
    woT = nc.dram_tensor("woT", [D, D], BF16, kind="Internal").ap()
    g_raw = nc.dram_tensor("g_raw", [II, D], BF16, kind="Internal").ap()
    u_raw = nc.dram_tensor("u_raw", [II, D], BF16, kind="Internal").ap()
    d_raw = nc.dram_tensor("d_raw", [D, II], BF16, kind="Internal").ap()
    wgT = nc.dram_tensor("wgT", [D, II], BF16, kind="Internal").ap()
    wuT = nc.dram_tensor("wuT", [D, II], BF16, kind="Internal").ap()
    wdT = nc.dram_tensor("wdT", [II, D], BF16, kind="Internal").ap()
    rs_hbm = nc.dram_tensor("rs_hbm", [NSEQ, NH, 4, 512], F32, kind="Internal").ap()
    ln1s = nc.dram_tensor("ln1s", [NSEQ, T, D], BF16, kind=test_kind).ap()
    ln3s = nc.dram_tensor("ln3s", [NSEQ, T, D], BF16, kind="Internal").ap()
    zts = nc.dram_tensor("zts", [NSEQ, NH, HD, T], BF16, kind=test_kind).ap()
    xn = nc.dram_tensor("xn", [NSEQ, T, D], F32, kind=test_kind).ap()
    hs = nc.dram_tensor("hs", [II, NSEQ * T], BF16, kind=test_kind).ap()
    fs = nc.dram_tensor("fs", [D, NSEQ * T], BF16, kind="Internal").ap()
    dbgv = nc.dram_tensor("dbgv", [NSEQ, NG, 128, NTT, 512], BF16,
                          kind="ExternalOutput" if lvl in (1, 2) else "Internal").ap()
    dbgqk = nc.dram_tensor("dbgqk", [NSEQ, NH, 2, 128, T], BF16,
                           kind="ExternalOutput" if lvl == 2 else "Internal").ap()

    gs_r = gs.rearrange("(a p one) -> a p one", p=128, one=1)
    us_r = us.rearrange("(a p one) -> a p one", p=128, one=1)
    ds_r = ds.rearrange("(a p one) -> a p one", p=128, one=1)
    wgT_r = wgT.rearrange("(c p) i -> p c i", p=128)
    wuT_r = wuT.rearrange("(c p) i -> p c i", p=128)
    wdT_r = wdT.rearrange("(S b p) h -> S p b h", b=8, p=128)
    woT_r = woT.rearrange("(c p) o -> p c o", p=128)
    wqT_r = wqT.rearrange("(c p) o -> p c o", p=128)
    wkT_r = wkT.rearrange("(c p) o -> p c o", p=128)
    wvT_r = wvT.rearrange("(c p) o -> p c o", p=128)
    zts_r = [zts[s].rearrange("h p t -> p h t") for s in range(NSEQ)]
    hs_r = hs.rearrange("(cb p) t -> p cb t", p=128)
    fs_r = fs.rearrange("(q ht p) t -> q p ht t", q=4, ht=4, p=128)
    fs_p = fs.rearrange("(pr fg p) t -> pr p fg t", pr=8, fg=2, p=128)

    spe = b.sem("spe"); sac = b.sem("sac"); sve = b.sem("sve"); spl = b.sem("spl")
    c_m = {k: b.sem(f"c_{k}") for k in "qkvo"}
    pa_ld = b.sem("pa_ld")
    pa_s = [b.sem(f"pa_s{i}") for i in range(2)]
    pa_wo = [b.sem(f"pa_wo{i}") for i in range(2)]
    pu_ld = b.sem("pu_ld")
    pu_s = [b.sem(f"pu_s{i}") for i in range(2)]
    ax = [b.sem(f"ax{i}") for i in range(2)]
    al_st = b.sem("al_st")
    al_tt = b.sem("al_tt")
    aw = b.sem("aw")
    az = b.sem("az")
    ars = b.sem("ars")
    aru = b.sem("aru")
    azl = [b.sem(f"azl{i}") for i in range(2)]
    awo = b.sem("awo")
    axr = [b.sem(f"axr{i}") for i in range(2)]
    axn = b.sem("axn")
    al3 = b.sem("al3")
    bl3t = b.sem("bl3t")
    bw = [b.sem(f"bw{i}") for i in range(2)]
    bh = b.sem("bh")
    b2h = [b.sem(f"b2h{i}") for i in range(2)]
    b2w = [b.sem(f"b2w{i}") for i in range(2)]
    bf = b.sem("bf")
    b3f = [b.sem(f"b3f{i}") for i in range(2)]
    b3x = [b.sem(f"b3x{i}") for i in range(2)]
    bo = b.sem("bo")
    bo2 = b.sem("bo2")
    dvq = b.sem("dvq")
    fin = b.sem("fin")

    es_root = ExitStack()
    for s in b.sems:
        s.h = es_root.enter_context(nc.semaphore(s.name))

    mask01 = es_root.enter_context(nc.sbuf_tensor("mask01", [128, 128], BF16))
    ones32 = es_root.enter_context(nc.sbuf_tensor("ones32", [128, 32], BF16))
    ones1 = es_root.enter_context(nc.sbuf_tensor("ones1", [1, 128], F32))
    onesT = es_root.enter_context(nc.sbuf_tensor("onesT", [128, 512], BF16))
    rsqT = es_root.enter_context(nc.sbuf_tensor("rsqT", [128, 512], BF16))
    eps_t = es_root.enter_context(nc.sbuf_tensor("eps_t", [128, 1], F32))
    gs_all = es_root.enter_context(nc.sbuf_tensor("gs_all", [128, 64], F32))
    us_all = es_root.enter_context(nc.sbuf_tensor("us_all", [128, 64], F32))
    ds_all = es_root.enter_context(nc.sbuf_tensor("ds_all", [128, 16], F32))
    cm128 = es_root.enter_context(nc.sbuf_tensor("cm128", [1, 128], BF16))
    p_tst = es_root.enter_context(nc.sbuf_tensor("p_tst", [128, 2 * D], BF16))
    stats = []
    for par in range(2):
        d = {}
        for nm in ["sum", "sumsq", "m", "m2", "var", "sd", "r", "nmr"]:
            d[nm] = es_root.enter_context(nc.sbuf_tensor(f"st_{nm}_{par}", [128, 1], F32))[:, :]
        d['eps'] = eps_t[:, :]
        stats.append(d)

    b.vop('ve', 'memset', eps_t[:], EPS)
    b.vop('ve', 'memset', ones32[:], 1.0)
    b.vop('ve', 'memset', ones1[:], 1.0)
    b.vop('ve', 'memset', onesT[:], 1.0)
    b.vop('ve', 'memset', rsqT[:], RSQ)
    b.vop('ve', 'memset', cm128[:], -128.0)
    b.vop('ve', 'memset', mask01[:], 1.0, inc=sve)
    v_mask_set = sve.n
    b.wait('gp', sve, v_mask_set)
    b.vop('gp', 'affine_select', mask01[:], mask01[:], [[1, 128]], ALU.is_ge, 0.0,
          base=0, channel_multiplier=-1, inc=spl)

    b.scope = 'prologue'
    for nm, (src, dst) in dict(q=(wq, wq_b), k=(wk, wk_b), v=(wv, wv_b), o=(wo, wo_b)).items():
        b.dma('gp', dst[:, :], src[:, :], c_m[nm])

    for nm, (srcb, dstT) in dict(q=(wq_b, wqT), k=(wk_b, wkT), v=(wv_b, wvT)).items():
        b.wait('sc', c_m[nm], 16)
        for c in range(NC):
            par = c % 2
            buf = p_tst[:, par * D:par * D + D]
            b.wait('sc', pa_s[par], pa_s[par].n)
            b.dma('sc', buf, srcb[:, c * 128:(c + 1) * 128], pa_ld, transpose=True)
            b.wait('sc', pa_ld, pa_ld.n)
            b.dma('sc', dstT[c * 128:(c + 1) * 128, :], buf, st_sems_unused if False else pa_s[par])
    v_paqkv = [pa_s[i].n for i in range(2)]

    pa_ld2 = b.sem("pa_ld2")
    pa_ld2.h = es_root.enter_context(nc.semaphore("pa_ld2"))

    def emit_wo_pairs():
        b.wait('sy', c_m['o'], 16)
        b.wait('sy', pa_s[0], v_paqkv[0])
        b.wait('sy', pa_s[1], v_paqkv[1])
        for c in range(NC):
            par = c % 2
            buf = p_tst[:, par * D:par * D + D]
            b.wait('sy', pa_wo[par], pa_wo[par].n)
            b.dma('sy', buf, wo_b[:, c * 128:(c + 1) * 128], pa_ld2, transpose=True)
            b.wait('sy', pa_ld2, pa_ld2.n)
            b.dma('sy', woT[c * 128:(c + 1) * 128, :], buf, pa_wo[par])

    c_f = {m: b.sem(f"c_f{m}") for m in "gud"}
    for m in c_f.values():
        m.h = es_root.enter_context(nc.semaphore(m.name))
    b.wait('gp', al_tt, 256)
    b.dma('gp', g_raw[:, :], gw[:, :], c_f['g'])
    b.dma('gp', u_raw[:, :], uw[:, :], c_f['u'])
    b.dma('gp', d_raw[:, :], dw[:, :], c_f['d'])
    scl_ld = b.sem("scl_ld")
    scl_ld.h = es_root.enter_context(nc.semaphore("scl_ld"))
    for blk in range(64):
        b.dma('gp', gs_all[:, blk:blk + 1], gs_r[blk], scl_ld)
        b.dma('gp', us_all[:, blk:blk + 1], us_r[blk], scl_ld)
    for blk in range(16):
        b.dma('gp', ds_all[:, blk:blk + 1], ds_r[blk], scl_ld)
    v_scl = scl_ld.n

    ffn_wt_tasks = []
    for mat, (dsrc, dstT) in dict(g=(g_raw, wgT), u=(u_raw, wuT)).items():
        for c in range(NC):
            for hf in range(2):
                ffn_wt_tasks.append((mat, dsrc[hf * 4096:(hf + 1) * 4096, c * 128:(c + 1) * 128],
                                     dstT[c * 128:(c + 1) * 128, hf * 4096:(hf + 1) * 4096], 4096))
    for ic in range(II // 128):
        ffn_wt_tasks.append(('d', d_raw[:, ic * 128:(ic + 1) * 128],
                             wdT[ic * 128:(ic + 1) * 128, :], D))
    wt_state = {"i": 0, "par": 0}
    esPump = ExitStack()
    p_tstf = esPump.enter_context(nc.sbuf_tensor("p_tstf", [128, 2 * 4096], BF16))

    def pump_ffn_wt(k):
        for _ in range(k):
            i = wt_state["i"]
            if i >= len(ffn_wt_tasks):
                return
            mat, src, dst, width = ffn_wt_tasks[i]
            par = wt_state["par"]
            buf = p_tstf[:, par * 4096:par * 4096 + width]
            b.wait('sy', pu_s[par], pu_s[par].n)
            b.wait('sy', c_f[mat], 16)
            b.dma('sy', buf, src, pu_ld, transpose=True)
            b.wait('sy', pu_ld, pu_ld.n)
            b.dma('sy', dst, buf, pu_s[par])
            wt_state["par"] = 1 - par
            wt_state["i"] = i + 1

    gate = dict(spe=0, sac=0, sve=0, spl=spl.n)

    def snap_gate():
        gate['spe'] = spe.n; gate['sac'] = sac.n; gate['sve'] = sve.n; gate['spl'] = spl.n

    v_xn_store = [0, 0]
    v_ln3_store = [0, 0]

    for s in range(NSEQ):
        esLn1T = ExitStack()
        ln1T = esLn1T.enter_context(nc.sbuf_tensor(f"ln1T{s}", [128, NC, T], BF16))
        esA0 = ExitStack()
        x_sb = [esA0.enter_context(nc.sbuf_tensor(f"x_sb{s}_{i}", [128, D], F32)) for i in range(2)]
        ln1tok = [esA0.enter_context(nc.sbuf_tensor(f"ln1tok{s}_{i}", [128, D], BF16)) for i in range(2)]
        sq_tr = esA0.enter_context(nc.sbuf_tensor(f"sq_tr{s}", [128, D], BF16))
        esA012 = ExitStack()

        b.wait('sy', spe, gate['spe']); b.wait('sy', sac, gate['sac'])
        b.wait('sy', sve, gate['sve'])
        if s > 0:
            b.wait('sy', az, az.n)
            b.wait('sy', axn, axn.n)
            b.wait('sy', al3, al3.n)

        b.scope = f'A0_{s}'
        v_norm_done = [0, 0]
        for ti in range(NTT):
            par = ti % 2
            st = stats[par]
            b.wait('sy', sve, v_norm_done[par])
            b.dma('sy', x_sb[par][:, :], x_in[s, ti * 128:(ti + 1) * 128, :], ax[par])
            vx = ax[par].n
            b.wait('ve', ax[par], vx)
            b.wait('sc', ax[par], vx)
            ln_stats(b, st, x_sb[par][:, :], sq_tr[:, :], D, (spe, sve, sac))
            b.wait('ve', al_st, al_st.n)
            b.vop('ve', 'tensor_scalar', ln1tok[par][:, :], x_sb[par][:, :],
                  st['r'], st['nmr'], ALU.mult, ALU.add, inc=sve)
            v_norm_done[par] = sve.n
            b.wait('sy', sve, v_norm_done[par])
            b.dma('sy', ln1s[s, ti * 128:(ti + 1) * 128, :], ln1tok[par][:, :], al_st)
        b.wait('sy', al_st, al_st.n)
        for c in range(NC):
            b.dma('sy', ln1T[:, c, :], ln1s[s][0:T, c * 128:(c + 1) * 128], al_tt, transpose=True)
        v_ln1T = al_tt.n
        esA0.close()

        if lvl == 0:
            esA012.close()
            esLn1T.close()
            snap_gate()
            continue

        b.scope = f'A12_{s}'
        wqkv_g = esA012.enter_context(nc.sbuf_tensor(f"wqkv{s}", [128, 3 * NC, 512], BF16))
        v_g = esA012.enter_context(nc.sbuf_tensor(f"v_g{s}", [128, NTT, 512], BF16))
        qT = esA012.enter_context(nc.sbuf_tensor(f"qT{s}", [128, T], BF16))
        kT = esA012.enter_context(nc.sbuf_tensor(f"kT{s}", [128, T], BF16))
        probsT = esA012.enter_context(nc.sbuf_tensor(f"probsT{s}", [128, T], BF16))
        zt_sb = esA012.enter_context(nc.sbuf_tensor(f"zt_sb{s}", [128, T], BF16))
        lnls_sb = esA012.enter_context(nc.sbuf_tensor(f"lnls{s}", [128, 512], F32))
        rss_sb = esA012.enter_context(nc.sbuf_tensor(f"rss{s}", [128, 512], F32))
        row_sb = esA012.enter_context(nc.sbuf_tensor(f"row{s}", [1, T], F32))
        rs_b = esA012.enter_context(nc.sbuf_tensor(f"rs_b{s}", [128, T], BF16))
        zt_raw = esA012.enter_context(nc.sbuf_tensor(f"zt_raw{s}", [128, T], BF16))
        esPA = ExitStack()
        zT_ps = esPA.enter_context(nc.psum_tensor(f"zT_ps{s}", [128, T], F32))
        pp = [esPA.enter_context(nc.psum_tensor(f"pp{s}_{i}", [128, 512], F32)) for i in range(2)]
        s_ps = esPA.enter_context(nc.psum_tensor(f"s_ps{s}", [128, 512], F32))

        v_pe_wdone = 0
        v_pe_vdone = 0
        v_ac_vevac_v = 0
        pp_last_exp = [0, 0]
        pp_last_exp_v = [0, 0]
        v_zevac_a2 = 0
        v_sps_free = 0
        v_pe_k1 = 0
        v_rsb_used = 0
        v_pe_qread = 0
        v_dbg_dump = 0
        av_done_span = [0, 0, 0, 0]
        pend_rs = None
        rs_state = {"v_rsb_used": 0, "v_sps_free": 0, "v_pe_k1": 0}

        def emit_rs_tail(p):
            nonlocal v_sps_free, v_pe_k1, v_rsb_used
            b.wait('te', aru, p['v_row'])
            b.wait('te', sac, p['v_lnls'])
            for bc in range(4):
                b.mm(s_ps[:, :], ones1[:, :], row_sb[0:1, bc * 512:(bc + 1) * 512],
                     start=True, stop=True, inc=spe, skip_gc=True)
                v_bc = spe.n
                b.wait('sc', spe, v_bc)
                b.wait('sc', sve, v_rsb_used)
                b.act(rs_b[:, bc * 512:(bc + 1) * 512], s_ps[:, :], AF.Copy, inc=sac)
                if bc < 3:
                    b.wait('te', sac, sac.n)
            v_rsb = sac.n
            v_sps_free = sac.n
            v_pe_k1 = spe.n
            b.wait('ve', sac, v_rsb)
            b.wait('ve', sac, p['v_zraw'])
            b.wait('ve', az, az.n)
            b.vop('ve', 'tensor_tensor', zt_sb[:, :], zt_raw[:, :], rs_b[:, :],
                  ALU.mult, inc=sve)
            v_zmul = sve.n
            v_rsb_used = v_zmul
            b.wait('sc', sve, v_zmul)
            b.dma('sc', zts[s, p['h'], :, :], zt_sb[:, :], az)

        for g in range(NG):
            b.wait('sy', spe, v_pe_wdone)
            b.wait('sy', al_tt, al_tt.n)
            b.wait('sy', pa_s[0], v_paqkv[0])
            b.wait('sy', pa_s[1], v_paqkv[1])
            for mi, wTr in enumerate([wqT_r, wkT_r, wvT_r]):
                b.dma('sy', wqkv_g[:, mi * NC:(mi + 1) * NC, :],
                      wTr[:, :, g * 512:(g + 1) * 512], aw)
            v_wload = aw.n
            if s == 0 and g == 0:
                emit_wo_pairs()
            b.wait('te', spe, v_pe_vdone)
            b.wait('te', aw, v_wload)
            for ti in range(NTT):
                ps = pp[ti % 2]
                b.wait('te', sac, pp_last_exp[ti % 2])
                for c in range(NC):
                    b.mm(ps[:, :], ln1T[:, c, ti * 128:(ti + 1) * 128],
                         wqkv_g[:, 2 * NC + c, :], start=(c == 0), stop=(c == NC - 1),
                         inc=spe if c == NC - 1 else None)
                v_vmm = spe.n
                b.wait('sc', spe, v_vmm)
                if ti == 0 and lvl in (1, 2):
                    b.wait('sc', dvq, dvq.n)
                b.act(v_g[:, ti, :], ps[:, :], AF.Copy, inc=sac)
                pp_last_exp[ti % 2] = sac.n
            v_ac_vevac = sac.n
            if lvl in (1, 2):
                b.wait('sc', sve, sve.n)
                v_dbg_dump = b.dma('sc', dbgv[s, g], v_g[:, :, :], dvq)
            if lvl == 1:
                v_pe_wdone = spe.n
                v_pe_vdone = spe.n
                continue

            for hh in range(GH):
                h = g * GH + hh
                b.wait('sc', spe, v_pe_qread)
                for (wi, dstT, scl) in [(0, qT, RSQ), (1, kT, 1.0)]:
                    for tc4 in range(4):
                        par = tc4 % 2
                        ps = pp[par]
                        b.wait('te', sac, pp_last_exp[par])
                        b.wait('te', sve, pp_last_exp_v[par])
                        colr = slice(tc4 * 512, (tc4 + 1) * 512)
                        for c in range(NC):
                            b.mm(ps[:, :],
                                 wqkv_g[:, wi * NC + c, hh * 128:(hh + 1) * 128],
                                 ln1T[:, c, colr],
                                 start=(c == 0), stop=(c == NC - 1),
                                 inc=spe if c == NC - 1 else None)
                        v_pmm = spe.n
                        b.wait('ve', spe, v_pmm)
                        b.vop('ve', 'tensor_tensor', dstT[:, colr], ps[:, :],
                              rsqT[:, :] if scl != 1.0 else onesT[:, :], ALU.mult, inc=sve)
                        pp_last_exp_v[par] = sve.n
                v_qk_evac_v = sve.n
                if lvl == 2:
                    b.wait('sc', sve, sve.n)
                    b.dma('sc', dbgqk[s, h, 0], qT[:, :], dvq)
                    v_dbg_dump = b.dma('sc', dbgqk[s, h, 1], kT[:, :], dvq)
                    b.wait('ve', dvq, v_dbg_dump)
                    v_pe_qread = spe.n
                    continue
                if pend_rs is not None:
                    emit_rs_tail(pend_rs)
                    pend_rs = None
                spans_all = []
                for j in range(NTT):
                    ci0 = j // 4
                    for ci in range(ci0, 4):
                        a = max(512 * ci, 128 * j)
                        spans_all.append((j, ci, a, 512 * (ci + 1) - a))
                nsp = len(spans_all)
                sc_mm = [0] * nsp
                exp_done = [0] * nsp
                msk_done = [0] * nsp

                def emit_scores(i):
                    j, ci, a, w = spans_all[i]
                    par = i % 2
                    b.wait('te', sac, pp_last_exp[par])
                    b.wait('te', sve, pp_last_exp_v[par])
                    if i == 0:
                        b.wait('te', sve, v_qk_evac_v)
                    b.mm(pp[par][:, 0:w], kT[:, j * 128:(j + 1) * 128], qT[:, a:a + w],
                         start=True, stop=True, inc=spe)
                    sc_mm[i] = spe.n

                def emit_exp(i):
                    j, ci, a, w = spans_all[i]
                    b.wait('sc', spe, max(sc_mm[i], av_done_span[ci]))
                    b.act(probsT[:, a:a + w], pp[i % 2][:, 0:w], AF.Exp, inc=sac)
                    pp_last_exp[i % 2] = sac.n
                    exp_done[i] = sac.n

                def emit_mask(i):
                    j, ci, a, w = spans_all[i]
                    b.wait('ve', sac, exp_done[i])
                    b.vop('ve', 'tensor_tensor', probsT[:, j * 128:(j + 1) * 128],
                          probsT[:, j * 128:(j + 1) * 128], mask01[:, :], ALU.mult,
                          inc=sve)
                    msk_done[i] = sve.n

                def emit_av(i, first):
                    j, ci, a, w = spans_all[i]
                    if first:
                        b.wait('te', sac, v_zevac_a2)
                        b.wait('te', sac, v_sps_free)
                        b.wait('te', sve, v_ac_vevac_v)
                    if msk_done[i]:
                        b.wait('te', sve, msk_done[i])
                    else:
                        b.wait('te', sac, exp_done[i])
                    b.mm(zT_ps[:, a:a + w], v_g[:, j, hh * 128:(hh + 1) * 128],
                         probsT[:, a:a + w], start=(j == 0), stop=(j == 4 * ci + 3),
                         inc=spe)
                    b.mm(s_ps[32 * ci:32 * ci + 32, (a - 512 * ci):(a - 512 * ci) + w],
                         ones32[:, :], probsT[:, a:a + w],
                         start=(j == 0), stop=(j == 4 * ci + 3),
                         inc=spe, tile_position=(0, 32 * ci), skip_gc=True)
                    av_done_span[ci] = spe.n

                emit_scores(0)
                for i in range(nsp):
                    if i + 1 < nsp:
                        emit_scores(i + 1)
                    emit_exp(i)
                    j, ci, a, w = spans_all[i]
                    if ci == j // 4:
                        emit_mask(i)
                    emit_av(i, first=(i == 0))
                v_pe_qread = spe.n
                v_attn_done = spe.n
                b.wait('sc', spe, v_attn_done)
                b.act(lnls_sb[:, :], s_ps[:, :], AF.Ln, inc=sac)
                v_lnls = sac.n
                b.wait('sc', sve, v_rsb_used)
                b.act(zt_raw[:, :], zT_ps[:, :], AF.Copy, inc=sac)
                v_zevac_a2 = sac.n
                b.wait('sc', sac, v_lnls)
                b.act(rss_sb[:, :], lnls_sb[:, :], AF.Exp, scale=-1.0, inc=sac)
                v_rss = sac.n
                b.wait('sc', sac, v_rss)
                b.dma('sc', rs_hbm[s, h], rss_sb[0:128:32, :], ars)
                b.wait('sc', ars, ars.n)
                b.wait('sc', spe, v_pe_k1)
                b.dma('sc', row_sb[0:1, :],
                      rs_hbm[s, h].rearrange("a t -> (a t)").rearrange("(one t) -> one t", one=1), aru)
                pend_rs = dict(h=h, v_row=aru.n, v_lnls=v_lnls, v_zraw=v_zevac_a2)
            v_pe_wdone = spe.n
            v_pe_vdone = spe.n
            pump_ffn_wt(12 if s == 0 else 20)

        if pend_rs is not None:
            emit_rs_tail(pend_rs)
            pend_rs = None
        esPA.close()
        esA012.close()
        esLn1T.close()

        if lvl <= 3:
            snap_gate()
            continue

        b.scope = f'A3_{s}'
        snap_gate()
        esA3 = ExitStack()
        woT_sb = esA3.enter_context(nc.sbuf_tensor(f"woT_sb{s}", [128, NC, D], BF16))
        zt_in = [esA3.enter_context(nc.sbuf_tensor(f"zt_in{s}_{i}", [128, NH, 128], BF16)) for i in range(2)]
        x_re = [esA3.enter_context(nc.sbuf_tensor(f"x_re{s}_{i}", [128, D], F32)) for i in range(2)]
        o_sb = esA3.enter_context(nc.sbuf_tensor(f"oA3_sb{s}", [128, D], F32))
        ln2_sb = esA3.enter_context(nc.sbuf_tensor(f"ln2_sb{s}", [128, D], F32))
        xn_sb = esA3.enter_context(nc.sbuf_tensor(f"xn_sb{s}", [128, D], F32))
        ln3tok = esA3.enter_context(nc.sbuf_tensor(f"ln3tok{s}", [128, D], BF16))
        sq_tr3 = esA3.enter_context(nc.sbuf_tensor(f"sq_tr3{s}", [128, D], BF16))
        esPA3 = ExitStack()
        o_ps = [esPA3.enter_context(nc.psum_tensor(f"o_ps{s}_{i}", [128, D], F32)) for i in range(2)]

        b.wait('sy', spe, gate['spe']); b.wait('sy', sac, gate['sac'])
        b.wait('sy', sve, gate['sve'])
        b.wait('sy', az, az.n)
        b.wait('te', sac, gate['sac'])
        q3 = 'sy' if s == 0 else 'gp'
        if s == 1:
            b.wait('gp', spe, gate['spe']); b.wait('gp', sve, gate['sve'])
            b.wait('gp', az, az.n)

        b.wait('sy', pa_wo[0], pa_wo[0].n)
        b.wait('sy', pa_wo[1], pa_wo[1].n)
        b.dma('sy', woT_sb[:, :, :], woT_r[:, :, :], awo)
        vwo = awo.n

        v_zload = [0, 0]
        v_ps_free = [0, 0]
        v_ps_free_v = [0, 0]
        v_xre_free = [0, 0]
        v_o_used = 0
        pend_st = [None]

        def flush_a3_st(s=s):
            p = pend_st[0]
            if p is None:
                return
            pend_st[0] = None
            b.wait('sy', sve, p['v_xn'])
            b.dma('sy', xn[p['s'], p['ti'] * 128:(p['ti'] + 1) * 128, :], xn_sb[:, :], axn)
            v_xn_store[p['s']] = axn.n
            b.wait('sy', sve, p['v_l3'])
            b.dma('sy', ln3s[p['s'], p['ti'] * 128:(p['ti'] + 1) * 128, :], ln3tok[:, :], al3)
            v_ln3_store[p['s']] = al3.n

        for ti in range(NTT):
            par = ti % 2
            st = stats[par]
            b.wait(q3, spe, v_zload[par])
            b.dma(q3, zt_in[par][:, :, :], zts_r[s][:, :, ti * 128:(ti + 1) * 128], azl[par])
            vz = azl[par].n
            b.wait(q3, sve, v_xre_free[par])
            b.dma(q3, x_re[par][:, :], x_in[s, ti * 128:(ti + 1) * 128, :], axr[par])
            vx = axr[par].n
            flush_a3_st()
            b.wait('te', azl[par], vz)
            b.wait('te', awo, vwo)
            b.wait('te', sac, v_ps_free[par])
            b.wait('te', sve, v_ps_free_v[par])
            for c in range(NH):
                for sp in range(4):
                    b.mm(o_ps[par][:, sp * 512:(sp + 1) * 512], zt_in[par][:, c, :],
                         woT_sb[:, c, sp * 512:(sp + 1) * 512],
                         start=(c == 0), stop=(c == NH - 1),
                         inc=spe if (c == NH - 1 and sp == 3) else None)
            v_omm = spe.n
            v_zload[par] = v_omm
            if lvl == 3.2:
                b.wait('ve', spe, v_omm)
                b.wait('ve', axn, axn.n)
                b.vop('ve', 'tensor_scalar', xn_sb[:, :], o_ps[par][:, :],
                      1.0, 0.0, ALU.mult, ALU.add, inc=sve)
                v_o = sve.n
                v_ps_free_v[par] = v_o
                b.wait('sy', sve, v_o)
                b.dma('sy', xn[s, ti * 128:(ti + 1) * 128, :], xn_sb[:, :], axn)
                v_xn_store[s] = axn.n
                continue
            b.wait('ve', spe, v_omm)
            b.wait('sc', spe, v_omm)
            b.wait('sc', sve, v_o_used)
            ln_stats(b, st, o_ps[par][:, :], sq_tr3[:, :], D, (spe, sve, sac),
                     evac_to=o_sb[:, :])
            v_ps_free[par] = sac.n
            b.vop('ve', 'tensor_scalar', ln2_sb[:, :], o_sb[:, :],
                  st['r'], st['nmr'], ALU.mult, ALU.add, inc=sve)
            v_o_used = sve.n
            b.wait('ve', axr[par], vx)
            b.wait('ve', axn, axn.n)
            b.wait('ve', sve, v_o_used)
            b.vop('ve', 'tensor_add', xn_sb[:, :], x_re[par][:, :], ln2_sb[:, :], inc=sve)
            v_xn = sve.n
            v_xre_free[par] = v_xn
            st2 = stats[1 - par]
            b.wait('sc', sve, v_xn)
            ln_stats(b, st2, xn_sb[:, :], sq_tr3[:, :], D, (spe, sve, sac))
            b.wait('ve', al3, al3.n)
            b.vop('ve', 'tensor_scalar', ln3tok[:, :], xn_sb[:, :],
                  st2['r'], st2['nmr'], ALU.mult, ALU.add, inc=sve)
            v_l3 = sve.n
            pend_st[0] = dict(s=s, ti=ti, v_xn=v_xn, v_l3=v_l3)
        flush_a3_st()
        esPA3.close()
        esA3.close()
        snap_gate()

    pump_ffn_wt(len(ffn_wt_tasks))

    if lvl <= 3:
        b.wait('gp', al_st, al_st.n)
        b.wait('gp', az, az.n)
        b.wait('gp', dvq, dvq.n)
        b.wait('gp', pu_s[0], pu_s[0].n)
        b.wait('gp', pu_s[1], pu_s[1].n)
        for s2 in range(NSEQ):
            b.dma('gp', out[s2], x_in[s2], fin)
        b.wait('gp', fin, fin.n)
        finalize(nc, b, es_root)
        return nc

    if lvl in (3.2, 4):
        b.wait('gp', axn, axn.n)
        b.wait('gp', al_st, al_st.n)
        b.wait('gp', pu_s[0], pu_s[0].n)
        b.wait('gp', pu_s[1], pu_s[1].n)
        for s2 in range(NSEQ):
            b.dma('gp', out[s2], xn[s2], fin)
        b.wait('gp', fin, fin.n)
        finalize(nc, b, es_root)
        return nc

    b.scope = 'B1'
    esB1 = ExitStack()
    ln3T = esB1.enter_context(nc.sbuf_tensor("ln3T", [128, NC, T], BF16))
    wg_sb = [esB1.enter_context(nc.sbuf_tensor(f"wg_sb{i}", [128, NC, 512], BF16)) for i in range(2)]
    wu_sb = [esB1.enter_context(nc.sbuf_tensor(f"wu_sb{i}", [128, NC, 512], BF16)) for i in range(2)]
    s_sb = esB1.enter_context(nc.sbuf_tensor("s_sb", [128, 2, 512], BF16))
    u_sb = esB1.enter_context(nc.sbuf_tensor("u_sb", [128, 2, 512], BF16))
    h_sb = [esB1.enter_context(nc.sbuf_tensor(f"h_sb{i}", [128, T], BF16)) for i in range(2)]
    esPB1 = ExitStack()
    gu_ps = [esPB1.enter_context(nc.psum_tensor(f"gu_ps{i}", [128, 512], F32)) for i in range(8)]

    b.wait('sy', spe, gate['spe']); b.wait('sy', sac, gate['sac'])
    b.wait('sy', sve, gate['sve'])
    b.wait('te', sac, gate['sac']); b.wait('te', sve, gate['sve'])
    b.wait('sy', axn, axn.n)
    b.wait('sy', al3, al3.n)
    b.wait('sy', az, az.n)
    b.wait('sy', pu_s[0], pu_s[0].n)
    b.wait('sy', pu_s[1], pu_s[1].n)
    b.wait('sc', scl_ld, v_scl)
    b.wait('ve', scl_ld, v_scl)

    NSUP = II // 512
    v_pe_wg_done = [0, 0]
    v_hstore = [0, 0]
    g_last_silu = [0, 0, 0, 0]
    u_last_mul = [0, 0, 0, 0]
    v_smul = [0, 0]
    for hf in range(NSEQ):
        b.wait('sy', al3, al3.n)
        b.wait('sy', spe, spe.n)
        for c in range(NC):
            b.dma('sy', ln3T[:, c, :], ln3s[hf][0:T, c * 128:(c + 1) * 128], bl3t, transpose=True)
        v_ln3T = bl3t.n
        for si in range(NSUP):
            sbuf_i = si % 2
            b.wait('sy', spe, v_pe_wg_done[sbuf_i])
            b.dma('sy', wg_sb[sbuf_i][:, :, :], wgT_r[:, :, si * 512:(si + 1) * 512], bw[sbuf_i])
            b.dma('sy', wu_sb[sbuf_i][:, :, :], wuT_r[:, :, si * 512:(si + 1) * 512], bw[sbuf_i])
            v_wg_load = bw[sbuf_i].n
            b.wait('te', bw[sbuf_i], v_wg_load)
            b.wait('te', bl3t, v_ln3T)
            for bb in range(4):
                hpar = bb % 2
                for tc in range(4):
                    par4 = (bb * 4 + tc) % 4
                    gp_ps = gu_ps[2 * par4]
                    up_ps = gu_ps[2 * par4 + 1]
                    b.wait('te', sac, g_last_silu[par4])
                    for c in range(NC):
                        b.mm(gp_ps[:, :], wg_sb[sbuf_i][:, c, bb * 128:(bb + 1) * 128],
                             ln3T[:, c, tc * 512:(tc + 1) * 512],
                             start=(c == 0), stop=(c == NC - 1),
                             inc=spe if c == NC - 1 else None)
                    v_gmm = spe.n
                    b.wait('te', sac, u_last_mul[par4])
                    for c in range(NC):
                        b.mm(up_ps[:, :], wu_sb[sbuf_i][:, c, bb * 128:(bb + 1) * 128],
                             ln3T[:, c, tc * 512:(tc + 1) * 512],
                             start=(c == 0), stop=(c == NC - 1),
                             inc=spe if c == NC - 1 else None)
                    v_umm = spe.n
                    spar = tc % 2
                    b.wait('sc', spe, v_gmm)
                    b.wait('sc', sve, v_smul[spar])
                    b.act(s_sb[:, spar, :], gp_ps[:, :], AF.Silu,
                          scale=gs_all[:, si * 4 + bb:si * 4 + bb + 1], inc=sac)
                    g_last_silu[par4] = sac.n
                    b.wait('sc', spe, v_umm)
                    b.act(u_sb[:, spar, :], up_ps[:, :], AF.Copy,
                          scale=us_all[:, si * 4 + bb:si * 4 + bb + 1], inc=sac)
                    u_last_mul[par4] = sac.n
                    v_si = sac.n
                    b.wait('ve', sac, v_si)
                    if tc == 0:
                        b.wait('ve', bh, bh.n)
                    b.vop('ve', 'tensor_tensor', h_sb[hpar][:, tc * 512:(tc + 1) * 512],
                          u_sb[:, spar, :], s_sb[:, spar, :], ALU.mult, inc=sve)
                    v_smul[spar] = sve.n
                v_hmul = sve.n
                b.wait('gp', sve, v_hmul)
                row0 = si * 512 + bb * 128
                b.dma('gp', hs[row0:row0 + 128, hf * T:(hf + 1) * T], h_sb[hpar][:, :], bh)
            v_pe_wg_done[sbuf_i] = spe.n
    esPB1.close()
    esB1.close()
    esPump.close()
    snap_gate()

    if stage == "b1":
        b.wait('gp', bh, bh.n)
        b.wait('gp', axn, axn.n)
        for s2 in range(NSEQ):
            b.dma('gp', out[s2], xn[s2], fin)
        b.wait('gp', fin, fin.n)
        finalize(nc, b, es_root)
        return nc

    b.scope = 'B2'
    esB3 = ExitStack()
    f_tok = [esB3.enter_context(nc.sbuf_tensor(f"f_tok{i}", [128, D], BF16)) for i in range(2)]
    xn_t = esB3.enter_context(nc.sbuf_tensor("xn_t", [128, D], F32))
    ln4_sb = esB3.enter_context(nc.sbuf_tensor("ln4_sb", [128, D], F32))
    o_sb = [esB3.enter_context(nc.sbuf_tensor(f"o_sb{i}", [128, D], F32)) for i in range(2)]
    v_o_used3 = [0, 0]
    sq_tr4 = esB3.enter_context(nc.sbuf_tensor("sq_tr4", [128, D], BF16))
    v_bf_oct = []
    v_f_used = [0, 0]
    v_x_used = [0]

    def emit_b3_tile(tt):
        s3, ti3 = tt // NTT, tt % NTT
        par3 = tt % 2
        st3 = stats[par3]
        b.wait('sy', bf, v_bf_oct[tt // 8])
        b.wait('sy', sve, v_f_used[par3])
        b.dma('sy', f_tok[par3][:, :], fs[0:D, tt * 128:(tt + 1) * 128], b3f[par3], transpose=True)
        vf3 = b3f[par3].n
        b.wait('sy', sve, v_x_used[0])
        b.dma('sy', xn_t[:, :], xn[s3, ti3 * 128:(ti3 + 1) * 128, :], b3x[0])
        vx3 = b3x[0].n
        b.wait('ve', b3f[par3], vf3)
        b.wait('sc', b3f[par3], vf3)
        ln_stats(b, st3, f_tok[par3][:, :], sq_tr4[:, :], D, (spe, sve, sac))
        b.vop('ve', 'tensor_scalar', ln4_sb[:, :], f_tok[par3][:, :],
              st3['r'], st3['nmr'], ALU.mult, ALU.add, inc=sve)
        v_f_used[par3] = sve.n
        bo_p = bo if par3 == 0 else bo2
        b.wait('ve', b3x[0], vx3)
        b.wait('ve', bo_p, v_o_used3[par3])
        b.wait('ve', sve, v_f_used[par3])
        b.vop('ve', 'tensor_add', o_sb[par3][:, :], xn_t[:, :], ln4_sb[:, :], inc=sve)
        v_add3 = sve.n
        v_x_used[0] = v_add3
        b.wait('sy', sve, v_add3)
        b.dma('sy', out[s3, ti3 * 128:(ti3 + 1) * 128, :], o_sb[par3][:, :], bo_p)
        v_o_used3[par3] = bo_p.n

    esB2 = ExitStack()
    hT_lo = esB2.enter_context(nc.sbuf_tensor("hT_lo", [128, 32, 1024], BF16))
    hT_hi = esB2.enter_context(nc.sbuf_tensor("hT_hi", [128, 32, 1024], BF16))
    hs_rlo = esB2.enter_context(nc.sbuf_tensor("hs_rlo", [1, 1024], BF16))
    hs_rhi = esB2.enter_context(nc.sbuf_tensor("hs_rhi", [1, 1024], BF16))
    wd_sb = [esB2.enter_context(nc.sbuf_tensor(f"wd_sb{i}", [128, 8, 256], BF16)) for i in range(2)]
    fT_sb = [esB2.enter_context(nc.sbuf_tensor(f"fT_sb{i}", [128, 2, 1024], BF16)) for i in range(2)]
    esPB2 = ExitStack()
    f_ps = [[esPB2.enter_context(nc.psum_tensor(f"f_ps{p}_{t}", [128, 1024], F32)) for t in range(2)] for p in range(2)]

    b.wait('sy', spe, gate['spe']); b.wait('sy', sac, gate['sac'])
    b.wait('sy', sve, gate['sve'])
    b.wait('sy', bh, bh.n)
    b.wait('sy', axn, axn.n)
    b.wait('te', sac, gate['sac']); b.wait('te', sve, gate['sve'])
    b.wait('sc', scl_ld, v_scl)
    b.wait('gp', spe, gate['spe']); b.wait('gp', sac, gate['sac'])
    b.wait('gp', sve, gate['sve'])
    b.wait('gp', bh, bh.n)

    NOCT = (NSEQ * T) // 1024
    v_pe_lo_done = 0
    v_pe_hi_done = 0
    v_pe_wd_done = [0, 0]
    v_f_evac = [0, 0]
    v_rlo_used = 0
    v_rhi_used = 0
    wchunk = 0
    for oc in range(NOCT):
        b.wait('sy', spe, v_pe_lo_done)
        b.dma('sy', hT_lo[:, :, :], hs_r[:, 0:32, oc * 1024:(oc + 1) * 1024], b2h[0])
        v_lo = b2h[0].n
        b.wait('sy', spe, v_pe_hi_done)
        b.dma('sy', hT_hi[:, :, :], hs_r[:, 32:64, oc * 1024:(oc + 1) * 1024], b2h[1])
        v_hi = b2h[1].n
        b.wait('te', b2h[0], v_lo)
        b.wait('te', sac, v_f_evac[0])
        for cb in range(32):
            for hf in range(2):
                b.mm(f_ps[0][0][0:1, hf * 512:(hf + 1) * 512], ones32[:, 0:1],
                     hT_lo[:, cb, hf * 512:(hf + 1) * 512],
                     start=(cb == 0), stop=(cb == 31),
                     inc=spe if (cb == 31 and hf == 1) else None)
        v_hsmm_lo = spe.n
        b.wait('ve', spe, v_hsmm_lo)
        b.wait('ve', spe, v_rlo_used)
        for hf in range(2):
            b.vop('ve', 'tensor_tensor', hs_rlo[0:1, hf * 512:(hf + 1) * 512],
                  f_ps[0][0][0:1, hf * 512:(hf + 1) * 512], onesT[0:1, 0:512],
                  ALU.mult, inc=sve if hf == 1 else None)
        v_rlo = sve.n
        for pr in range(8):
            par = pr % 2
            b.wait('te', sac, v_f_evac[par])
            b.wait('te', sve, v_rlo)
            if pr == 1:
                b.wait('te', sve, v_rhi)
            for fg2 in range(2):
                for hf in range(2):
                    b.mm(f_ps[par][fg2][:, hf * 512:(hf + 1) * 512], cm128[:, :],
                         hs_rlo[0:1, hf * 512:(hf + 1) * 512],
                         start=True, stop=False, skip_gc=True)
            for S in range(8):
                wbuf = wchunk % 2
                wchunk += 1
                b.wait('gp', spe, v_pe_wd_done[wbuf])
                b.dma('gp', wd_sb[wbuf][:, :, :], wdT_r[S][:, :, pr * 256:(pr + 1) * 256], b2w[wbuf])
                v_wd_load = b2w[wbuf].n
                if pr == 0 and S == 4:
                    b.wait('te', b2h[1], v_hi)
                    b.wait('te', sac, v_f_evac[1])
                    for cb in range(32):
                        for hf in range(2):
                            b.mm(f_ps[1][0][0:1, hf * 512:(hf + 1) * 512], ones32[:, 0:1],
                                 hT_hi[:, cb, hf * 512:(hf + 1) * 512],
                                 start=(cb == 0), stop=(cb == 31),
                                 inc=spe if (cb == 31 and hf == 1) else None)
                    v_hsmm_hi = spe.n
                    b.wait('ve', spe, v_hsmm_hi)
                    b.wait('ve', spe, v_rhi_used)
                    for hf in range(2):
                        b.vop('ve', 'tensor_tensor', hs_rhi[0:1, hf * 512:(hf + 1) * 512],
                              f_ps[1][0][0:1, hf * 512:(hf + 1) * 512], onesT[0:1, 0:512],
                              ALU.mult, inc=sve if hf == 1 else None)
                    v_rhi = sve.n
                if S == 4:
                    b.wait('te', sve, v_rhi)
                    for fg2 in range(2):
                        for hf in range(2):
                            b.mm(f_ps[par][fg2][:, hf * 512:(hf + 1) * 512], cm128[:, :],
                                 hs_rhi[0:1, hf * 512:(hf + 1) * 512],
                                 start=False, stop=False, skip_gc=True)
                b.wait('te', b2w[wbuf], v_wd_load)
                hsrc = hT_lo if S < 4 else hT_hi
                for ib in range(8):
                    cb = (S % 4) * 8 + ib
                    last = (S == 7 and ib == 7)
                    for fg2 in range(2):
                        for hf in range(2):
                            b.mm(f_ps[par][fg2][:, hf * 512:(hf + 1) * 512],
                                 wd_sb[wbuf][:, ib, fg2 * 128:(fg2 + 1) * 128],
                                 hsrc[:, cb, hf * 512:(hf + 1) * 512],
                                 start=False, stop=last,
                                 inc=spe if (ib == 7 and fg2 == 1 and hf == 1) else None,
                                 skip_gc=True)
                v_pe_wd_done[wbuf] = spe.n
                if pr == 7 and S == 3:
                    v_pe_lo_done = spe.n
                    v_rlo_used = spe.n
            v_dmm = spe.n
            if pr == 7:
                v_pe_hi_done = v_dmm
                v_rhi_used = v_dmm
            b.wait('sc', spe, v_dmm)
            b.wait('sc', bf, bf.n)
            for fg2 in range(2):
                for hf in range(2):
                    b.act(fT_sb[par][:, fg2, hf * 512:(hf + 1) * 512],
                          f_ps[par][fg2][:, hf * 512:(hf + 1) * 512], AF.Copy,
                          scale=ds_all[:, pr * 2 + fg2:pr * 2 + fg2 + 1],
                          inc=sac if (fg2 == 1 and hf == 1) else None)
            v_f_evac[par] = sac.n
            b.wait('sc', sac, v_f_evac[par])
            b.dma('sc', fs_p[pr][:, :, oc * 1024:(oc + 1) * 1024], fT_sb[par][:, :, :], bf)
            if oc >= 1:
                emit_b3_tile(8 * (oc - 1) + pr)
        v_bf_oct.append(bf.n)
    esPB2.close()
    esB2.close()
    snap_gate()

    b.scope = 'B3'
    for tt in range(24, 32):
        emit_b3_tile(tt)

    b.wait('sy', bo, bo.n)
    b.wait('sy', bo2, bo2.n)
    esB3.close()
    finalize(nc, b, es_root)
    return nc


def finalize(nc, b, es_root):
    from contextlib import ExitStack as _ES
    with nc.Block() as block:
        def emit(name):
            def run(e):
                cur = None
                stk = _ES()
                for eng, fn, sc in b.plan:
                    if eng == name:
                        if sc != cur:
                            stk.close()
                            stk = _ES()
                            stk.enter_context(nc.named_scope(sc))
                            cur = sc
                        fn(e)
                stk.close()
            return run
        block.sync(emit('sy'))
        block.scalar(emit('sc'))
        block.gpsimd(emit('gp'))
        block.vector(emit('ve'))
        block.tensor(emit('te'))
    es_root.close()


def _in_map(inputs, core):
    x = np.ascontiguousarray(np.asarray(inputs["L0_attn_x"])[core * 2:(core + 1) * 2])
    return {
        "x": x,
        "wq": np.asarray(inputs["L0_attn_Wq"]),
        "wk": np.asarray(inputs["L0_attn_Wk"]),
        "wv": np.asarray(inputs["L0_attn_Wv"]),
        "wo": np.asarray(inputs["L0_attn_Wo"]),
        "gw": np.asarray(inputs["L0_ffn_gate_w"]),
        "uw": np.asarray(inputs["L0_ffn_up_w"]),
        "dw": np.asarray(inputs["L0_ffn_down_w"]),
        "gs": np.asarray(inputs["L0_ffn_gate_scale"]),
        "us": np.asarray(inputs["L0_ffn_up_scale"]),
        "ds": np.asarray(inputs["L0_ffn_down_scale"]),
    }


def kernel(**inputs):
    nc = build_nc("full")
    in_maps = [_in_map(inputs, c) for c in range(8)]
    res = run_bass_kernel_spmd(nc, in_maps, core_ids=list(range(8)))
    return np.concatenate([r["out"] for r in res.results], axis=0)
